# version 11
# baseline (speedup 1.0000x reference)
import numpy as np
from contextlib import ExitStack

import concourse.bass as bass
import concourse.mybir as mybir
from concourse.bass_utils import run_bass_kernel_spmd

F32 = mybir.dt.float32
I32 = mybir.dt.int32
AF = mybir.ActivationFunctionType
OP = mybir.AluOpType
AX = mybir.AxisListType

NCORES = 8
D = 2048
T = 1024
H = 16
DK = 128
CS = 64
NCH = T // CS
NT = T // 128
KC = D // 128
A1W, A2W, A3W = 16384, 16512, 6144


def _split3(v):
    a = np.float32(v)
    b = np.float32(v - float(a))
    c = np.float32(v - float(a) - float(b))
    return float(a), float(b), float(c)


TWO_PI = 2 * np.pi
C1 = 6.28125
C2 = float(np.float32(TWO_PI - C1))
C3 = float(np.float32(TWO_PI - C1 - C2))
INV_2PI = float(np.float32(1.0 / TWO_PI))
L1, L2, L3 = _split3(np.log(10000.0) / 64.0)
HPI1, HPI2, _ = _split3(np.pi / 2)


class _Ctr:
    def __init__(self):
        self.n = 0


def _build(stage=5):
    nc = bass.Bass("TRN2", target_bir_lowering=False, detect_race_conditions=False)

    x_t = nc.dram_tensor("x_t", [D, T], F32, kind="ExternalInput").ap()
    wq = nc.dram_tensor("wq", [D, D], F32, kind="ExternalInput").ap()
    wk = nc.dram_tensor("wk", [D, D], F32, kind="ExternalInput").ap()
    wv = nc.dram_tensor("wv", [D, D], F32, kind="ExternalInput").ap()
    wg = nc.dram_tensor("wg", [D, D], F32, kind="ExternalInput").ap()
    wo = nc.dram_tensor("wo", [D, D], F32, kind="ExternalInput").ap()
    fw = nc.dram_tensor("fw", [D, D], F32, kind="ExternalInput").ap()
    biases = {
        nm: nc.dram_tensor("b" + nm, [D], F32, kind="ExternalInput").ap()
        for nm in ("q", "k", "v", "g", "o", "f")
    }
    pos_d = nc.dram_tensor("pos", [1, T], F32, kind="ExternalInput").ap()
    jrow_d = nc.dram_tensor("jrow", [1, 64], F32, kind="ExternalInput").ap()
    ident_d = nc.dram_tensor("ident", [128, 128], F32, kind="ExternalInput").ap()
    pmask_d = nc.dram_tensor("pmask", [1, NCORES], F32, kind="ExternalInput").ap()

    y_t = nc.dram_tensor("y_t", [D, T], F32, kind="ExternalOutput").ap()

    q_sp = nc.dram_tensor("q_sp", [H * NT * 128, 128], F32).ap()
    g_sp = nc.dram_tensor("g_sp", [H * NT * 128, 128], F32).ap()
    k_sp = nc.dram_tensor("k_sp", [NT * 128, D], F32).ap()
    v_sp = nc.dram_tensor("v_sp", [NT * 128, D], F32).ap()
    wt_sp = nc.dram_tensor("wt_sp", [D, D], F32).ap()
    bc_d = nc.dram_tensor("bc_d", [1, 8192], F32).ap()
    cc_in = nc.dram_tensor("cc_in", [D, 129], F32)
    cc_out = nc.dram_tensor("cc_out", [NCORES * D, 129], F32)

    dbg = {}
    if stage < 5:
        dbg["tabs"] = nc.dram_tensor("dbg_tabs", [128, 1024], F32, kind="ExternalOutput").ap()
        dbg["xh"] = nc.dram_tensor("dbg_xh", [D, T], F32, kind="ExternalOutput").ap()
        dbg["k"] = nc.dram_tensor("dbg_k", [NT * 128, D], F32, kind="ExternalOutput").ap()
        dbg["v"] = nc.dram_tensor("dbg_v", [NT * 128, D], F32, kind="ExternalOutput").ap()
        dbg["q"] = nc.dram_tensor("dbg_q", [H * NT * 128, 128], F32, kind="ExternalOutput").ap()
        dbg["g"] = nc.dram_tensor("dbg_g", [H * NT * 128, 128], F32, kind="ExternalOutput").ap()
    if stage in (2, 3, 4):
        dbg["S"] = nc.dram_tensor("dbg_S", [D, 129], F32, kind="ExternalOutput").ap()
        dbg["Ssb"] = nc.dram_tensor("dbg_Ssb", [D, 129], F32, kind="ExternalOutput").ap()
    if stage in (3, 4):
        dbg["attn"] = nc.dram_tensor("dbg_attn", [D, T], F32, kind="ExternalOutput").ap()
        dbg["num"] = nc.dram_tensor("dbg_num", [64, 129 * 4], F32, kind="ExternalOutput").ap()
        dbg["h0"] = nc.dram_tensor("dbg_h0", [128, 256], F32, kind="ExternalOutput").ap()
    if stage == 4:
        dbg["x2"] = nc.dram_tensor("dbg_x2", [D, T], F32, kind="ExternalOutput").ap()
    if stage == 6:
        dbg["ffn"] = nc.dram_tensor("dbg_ffn", [128, 8], F32, kind="ExternalOutput").ap()

    prog = {e: [] for e in ("sync", "tensor", "vector", "scalar", "gpsimd")}
    sL, sG, sT, sV, sS, sC = (_Ctr() for _ in range(6))
    sem_h = {}
    waited = {}

    def emit(eng, fn):
        prog[eng].append(fn)

    def wait(eng, ctr, val=None):
        v = ctr.n if val is None else val
        if v <= 0:
            return
        key = (eng, id(ctr))
        if waited.get(key, -1) >= v:
            return
        waited[key] = v
        emit(eng, lambda e, v=v, c=ctr: e.wait_ge(sem_h[id(c)], v))

    def dma(eng, ctr, out, in_, slow=False):
        ctr.n += 16
        emit(eng, lambda e, out=out, in_=in_, c=ctr, s=slow:
             e.dma_start(out=out, in_=in_, allow_slow_non_contiguous=s)
             .then_inc(sem_h[id(c)], 16))
        return ctr.n

    fence_scratch = {}

    def op(eng, ctr, meth, *a, **kw):
        fence = kw.pop("fence", True)
        if ctr is None:
            emit(eng, lambda e, meth=meth, a=a, kw=kw: getattr(e, meth)(*a, **kw))
            return None
        ctr.n += 1
        if not fence or eng in ("sync", "tensor"):
            def f(e, meth=meth, a=a, kw=kw, c=ctr):
                getattr(e, meth)(*a, **kw).then_inc(sem_h[id(c)], 1)
            emit(eng, f)
            return ctr.n
        emit(eng, lambda e, meth=meth, a=a, kw=kw: getattr(e, meth)(*a, **kw))
        fa, fb = fence_scratch[eng]

        def g(e, c=ctr, eng=eng, fa=fa, fb=fb):
            if eng == "scalar":
                ins = e.activation(fb, fa, AF.Identity)
            else:
                ins = e.tensor_copy(fb, fa)
            ins.then_inc(sem_h[id(c)], 1)
        emit(eng, g)
        return ctr.n

    bank_rd = {}

    def use_bank(b):
        g = bank_rd.get(b)
        if g:
            wait("tensor", g[0], g[1])

    def rd_bank(b, ctr, val):
        bank_rd[b] = (ctr, val)

    def pad(eng):
        fa, fb = pad_scratch[eng]
        if eng == "scalar":
            emit(eng, lambda e, fa=fa, fb=fb: e.activation(fb, fa, AF.Identity))
        else:
            emit(eng, lambda e, fa=fa, fb=fb: e.tensor_copy(fb, fa))

    with ExitStack() as ctx:
        def sbuf(name, shape, dt=F32):
            return ctx.enter_context(nc.sbuf_tensor(name, shape, dt))

        A1 = sbuf("A1", [128, A1W])
        A2 = sbuf("A2", [128, A2W])
        A3 = sbuf("A3", [128, A3W])
        kint = sbuf("kint", [128, 64], I32)
        stm = sbuf("stm", [128, 512])
        ctm = sbuf("ctm", [128, 512])
        S_all = sbuf("S_all", [128, H * 129])
        S_sb = sbuf("S_sb", [128, H * 129])
        A4 = sbuf("A4", [128, 2048])
        t11s = sbuf("t11s", [1, 1])
        nscc = sbuf("nscc", [128, 1])
        bcol = sbuf("bcol", [128, 96])
        irow = sbuf("irow", [1, 64])
        s_e = sbuf("s_e", [1, 64])
        ones_r = sbuf("ones_r", [1, 128])
        ones_c = sbuf("ones_c", [128, 1])
        idn = sbuf("idn", [128, 128])
        posc = sbuf("posc", [128, NT])
        ivfb = sbuf("ivfb", [128, 64])
        atm = sbuf("atm", [128, 64])
        atmc = sbuf("atmc", [128, 64])
        mtm = sbuf("mtm", [128, 64])
        mtk = sbuf("mtk", [128, 64])
        mta = sbuf("mta", [128, 64])
        accc = sbuf("accc", [128, 1])
        rcol = sbuf("rcol", [128, 1])
        rcol2 = sbuf("rcol2", [128, 1])
        thc = sbuf("thc", [128, 1])
        nthc = sbuf("nthc", [128, 1])
        scc = sbuf("scc", [128, 1])
        t11a = sbuf("t11a", [1, 1])
        t11b = sbuf("t11b", [1, 1])
        t11n = sbuf("t11n", [1, 1])
        tot = sbuf("tot", [1, 1])
        trow = sbuf("trow", [1, 128])
        pmc = sbuf("pmc", [128, 1])
        c_eps = sbuf("c_eps", [1, 1])
        c_e11 = sbuf("c_e11", [1, 1])
        dnall = sbuf("dnall", [64, 16])
        rcall = sbuf("rcall", [64, 16])
        dnc = sbuf("dnc", [64, 1])
        rcc = sbuf("rcc", [64, 1])
        ftmp = sbuf("ftmp", [128, 1024])
        if stage in (3, 4):
            dbg_num_sb = sbuf("dbg_num_sb", [64, 129 * 4])
            dbg_h0_sb = sbuf("dbg_h0_sb", [128, 256])
        if stage in (3, 4):
            pass
        pad_scratch = {}
        for _e in ("vector", "scalar", "gpsimd"):
            _fa = sbuf("fence_a_" + _e, [1, 1])
            _fb = sbuf("fence_b_" + _e, [1, 1])
            fence_scratch[_e] = (_fa[:], _fb[:])
            _pa = sbuf("pad_a_" + _e, [64, 64])
            _pb2 = sbuf("pad_b_" + _e, [64, 64])
            pad_scratch[_e] = (_pa[:], _pb2[:])

        pb = [ctx.enter_context(nc.psum_tensor("pb%d" % i, [128, 512], F32))
              for i in range(8)]

        for name, c in (("semL", sL), ("semG", sG), ("semT", sT),
                        ("semV", sV), ("semS", sS), ("semC", sC)):
            sem_h[id(c)] = ctx.enter_context(nc.semaphore(name))

        def a1(off, n):
            return bass.AP(A1[:].tensor, off, [[A1W, 128], [1, n]])

        def a2(off, n, p=128, pbase=0):
            return bass.AP(A2[:].tensor, off + pbase * A2W, [[A2W, p], [1, n]])

        def a3(off, n):
            return bass.AP(A3[:].tensor, off, [[A3W, 128], [1, n]])

        def a3row(off, n=512):
            return bass.AP(A3[:].tensor, off, [[A3W, 1], [1, n]])

        def ap_sub(ap, doff, n):
            return bass.AP(ap.tensor, ap.offset + doff, [list(ap.ap[0]), [1, n]])

        def dram_ap(d, offset, dims):
            return bass.AP(d.tensor, offset, dims)

        def finish():
            with nc.Block() as blk:
                for name, fns in prog.items():
                    def run_all(e, fns=fns):
                        for f in fns:
                            f(e)
                    getattr(blk, name)(run_all)

        dma("sync", sL, idn[:], ident_d)
        for i, nm in enumerate(("q", "k", "v", "g", "o", "f")):
            dma("sync", sL, bass.AP(bcol[:].tensor, i * 16, [[96, 128], [1, 16]]),
                dram_ap(biases[nm], 0, [[1, 128], [128, 16]]), slow=True)
        dma("sync", sL, posc[:], dram_ap(pos_d, 0, [[1, 128], [128, NT]]), slow=True)
        p0_loads = sL.n

        op("vector", None, "memset", ones_r[:], 1.0)
        op("vector", None, "memset", ones_c[:], 1.0)
        op("vector", None, "memset", c_eps[:], 1e-5)
        op("vector", None, "memset", c_e11[:], 5e-11)
        for _e in ("vector", "scalar", "gpsimd"):
            op("vector", None, "memset", fence_scratch[_e][0], 0.0)
            op("vector", None, "memset", pad_scratch[_e][0], 0.0)
        if stage in (3, 4):
            op("vector", None, "memset", dbg_h0_sb[:], 0.0)
            op("vector", None, "memset", dbg_num_sb[:], 0.0)
        jld = dma("sync", sL, s_e[:], jrow_d)
        wait("vector", sL, jld)
        op("vector", None, "tensor_scalar", irow[:], s_e[:], -L1, None, op0=OP.mult)
        op("vector", None, "scalar_tensor_tensor", irow[:], s_e[:], -L2, irow[:],
           op0=OP.mult, op1=OP.add)
        v_n = op("vector", sV, "scalar_tensor_tensor", irow[:], s_e[:], -L3, irow[:],
                 op0=OP.mult, op1=OP.add)
        wait("scalar", sV, v_n)
        s_n = op("scalar", sS, "activation", irow[:], irow[:], AF.Exp)
        ivf_done = s_n

        def emit_row_bcast(row_ap, out_ap, n, boff, prod_ctr, prod_val):
            wait("sync", prod_ctr, prod_val)
            l1 = dma("sync", sL, dram_ap(bc_d, boff, [[1, 1], [1, n]]), row_ap)
            wait("sync", sL, l1)
            return dma("sync", sL, out_ap,
                       bass.AP(bc_d.tensor, boff, [[0, out_ap.ap[0][1]], [1, n]]))

        ivfb_ld = emit_row_bcast(irow[:], ivfb[:], 64, 0, sS, ivf_done)
        wait("vector", sL, ivfb_ld)

        wait("vector", sL, p0_loads)
        for tau in range(NT):
            wait("vector", sS, sS.n)
            op("vector", None, "tensor_scalar", atm[:], ivfb[:], posc[:, tau:tau + 1],
               None, op0=OP.mult)
            op("vector", None, "tensor_scalar", atmc[:], atm[:], HPI1, HPI2,
               op0=OP.add, op1=OP.add)
            for src, dstt in ((atm, stm), (atmc, ctm)):
                op("vector", None, "tensor_scalar", mtm[:], src[:], INV_2PI, None, op0=OP.mult)
                op("vector", None, "tensor_copy", kint[:], mtm[:])
                op("vector", None, "tensor_copy", mtk[:], kint[:])
                op("vector", None, "tensor_tensor", mtm[:], mtm[:], mtk[:], op=OP.subtract)
                op("vector", None, "tensor_scalar", mta[:], mtm[:], 0.499999, None, op0=OP.is_gt)
                op("vector", None, "tensor_tensor", mtk[:], mtk[:], mta[:], op=OP.add)
                op("vector", None, "tensor_scalar", mta[:], mtm[:], -0.499999, None, op0=OP.is_lt)
                op("vector", None, "tensor_tensor", mtk[:], mtk[:], mta[:], op=OP.subtract)
                op("vector", None, "scalar_tensor_tensor", src[:], mtk[:], -C1, src[:],
                   op0=OP.mult, op1=OP.add)
                op("vector", None, "scalar_tensor_tensor", src[:], mtk[:], -C2, src[:],
                   op0=OP.mult, op1=OP.add)
                op("vector", None, "scalar_tensor_tensor", src[:], mtk[:], -C3, src[:],
                   op0=OP.mult, op1=OP.add)
                v_n = op("vector", sV, "tensor_scalar", src[:], src[:], 3.141592,
                         -3.141592, op0=OP.min, op1=OP.max)
                wait("scalar", sV, v_n)
                op("scalar", sS, "activation", dstt[:, tau * 64:(tau + 1) * 64], src[:], AF.Sin)
        tables_done = sS.n
        if stage in (-1, 0):
            wait("gpsimd", sS, tables_done)
            dma("gpsimd", sG, dram_ap(dbg["tabs"], 0, [[1024, 128], [1, 512]]), stm[:])
            dma("gpsimd", sG, dram_ap(dbg["tabs"], 512, [[1024, 128], [1, 512]]), ctm[:])
        if stage == -1:
            wait("gpsimd", sG)
            finish()
            return nc

        def emit_ln(src_tile_fn, dst_off_fn, xbuf, sq, srow, ab, from_dram, bc_base):
            mm_lm = [0] * KC
            sq_lm = [0] * KC
            for k in range(KC):
                if from_dram:
                    if k >= 2:
                        wait("sync", sT, mm_lm[k - 2])
                    ld = dma("sync", sL, xbuf[k % 2], src_tile_fn(k))
                src_ap = xbuf[k % 2] if from_dram else src_tile_fn(k)
                if k >= 2:
                    wait("scalar", sT, mm_lm[k - 2])
                if from_dram:
                    wait("scalar", sL, ld)
                sq_lm[k] = op("scalar", sS, "activation", sq[k % 2], src_ap, AF.Square)
                wait("tensor", sS, sq_lm[k])
                if k == 0:
                    for b in range(4):
                        use_bank(b)
                for th in range(2):
                    op("tensor", None, "matmul", pb[th][0:1, 0:512], ones_c[:],
                       ap_sub(src_ap, th * 512, 512), start=(k == 0), stop=(k == KC - 1))
                op("tensor", None, "matmul", pb[2][0:1, 0:512], ones_c[:],
                   ap_sub(sq[k % 2], 0, 512), start=(k == 0), stop=(k == KC - 1))
                mm_lm[k] = op("tensor", sT, "matmul", pb[3][0:1, 0:512], ones_c[:],
                              ap_sub(sq[k % 2], 512, 512), start=(k == 0), stop=(k == KC - 1))
            wait("vector", sT, mm_lm[KC - 1])
            for th in range(2):
                op("vector", None, "tensor_scalar", srow(th * 512, 512), pb[th][0:1, 0:512],
                   1.0 / D, None, op0=OP.mult)
                op("vector", None, "tensor_tensor", srow(1024 + th * 512, 512),
                   srow(th * 512, 512), srow(th * 512, 512), op=OP.mult)
                v_n = op("vector", sV, "scalar_tensor_tensor", srow(2048 + th * 512, 512),
                         pb[2 + th][0:1, 0:512], 1.0 / D, srow(1024 + th * 512, 512),
                         op0=OP.mult, op1=OP.subtract)
                rd_bank(th, sV, v_n)
                rd_bank(2 + th, sV, v_n)
                wait("scalar", sV, v_n)
                s_n = op("scalar", sS, "activation", srow(1024 + th * 512, 512),
                         srow(2048 + th * 512, 512), AF.Sqrt, bias=c_eps[:])
                wait("vector", sS, s_n)
                op("vector", None, "reciprocal", srow(2048 + th * 512, 512),
                   srow(1024 + th * 512, 512))
                op("vector", sV, "scalar_tensor_tensor", srow(3072 + th * 512, 512),
                   srow(th * 512, 512), -1.0, srow(2048 + th * 512, 512),
                   op0=OP.mult, op1=OP.mult)
            ab_lds = []
            for i, soff in enumerate((2048, 3072)):
                for th in range(2):
                    ab_lds.append(emit_row_bcast(
                        srow(soff + th * 512, 512), ap_sub(ab[i], th * 512, 512),
                        512, bc_base + (i * 2 + th) * 512, sV, sV.n))
            wait("vector", sL, ab_lds[-1])
            if from_dram:
                wait("sync", sT, mm_lm[KC - 1])
            lnb_lm = [0] * KC
            for k in range(KC):
                if from_dram:
                    if k >= 2:
                        wait("sync", sV, lnb_lm[k - 2])
                    ld = dma("sync", sL, xbuf[k % 2], src_tile_fn(k))
                    wait("vector", sL, ld)
                src_ap = xbuf[k % 2] if from_dram else src_tile_fn(k)
                lnb_lm[k] = op("vector", sV, "tensor_tensor", dst_off_fn(k),
                               src_ap, ab[0], op=OP.mult)
            for k in range(KC):
                op("vector", sV if k == KC - 1 else None, "tensor_tensor",
                   dst_off_fn(k), dst_off_fn(k), ab[1], op=OP.add)
            return sV.n

        def x_tile(k):
            return dram_ap(x_t, k * 128 * T, [[T, 128], [1, T]])

        wait("tensor", sS, tables_done)
        ln1_done = emit_ln(
            x_tile, lambda k: a1(k * 1024, 1024),
            [a2(0, 1024), a2(1024, 1024)],
            [a2(2048, 1024), a2(3072, 1024)],
            lambda off, n=512: a3row(off, n),
            [a2(2048, 1024), a2(3072, 1024)],
            True, 1024,
        )
        if stage < 5:
            wait("gpsimd", sV, ln1_done)
            for k in range(KC):
                dma("gpsimd", sG, dram_ap(dbg["xh"], k * 128 * T, [[T, 128], [1, T]]),
                    a1(k * 1024, 1024))
        if stage == 0:
            wait("gpsimd", sG)
            finish()
            return nc

        def wslot(k, n=512):
            return a2(k * 512, n)

        def wks(tp, i, n=512):
            return a2(8192 + tp * 2048 + i * 512, n)

        def emit_bias_bcast(bias_dram):
            ld = dma("sync", sL, a3(0, 2048),
                     bass.AP(bias_dram.tensor, 0, [[0, 128], [1, 2048]]))
            return ld

        def emit_rope_elu_tm(wa, wb, wc, wd, tau):
            sslice = stm[:, tau * 64:(tau + 1) * 64]
            cslice = ctm[:, tau * 64:(tau + 1) * 64]
            for hh in range(4):
                c0 = hh * 128
                k1 = ap_sub(wa, c0, 64)
                k2 = ap_sub(wa, c0 + 64, 64)
                op("vector", None, "tensor_tensor", ap_sub(wb, c0, 64), k1, cslice, op=OP.mult)
                op("vector", None, "tensor_tensor", ap_sub(wb, c0 + 64, 64), k1, sslice, op=OP.mult)
                op("vector", None, "tensor_tensor", ap_sub(wc, c0, 64), k2, sslice, op=OP.mult)
                op("vector", None, "tensor_tensor", ap_sub(wc, c0 + 64, 64), k2, cslice, op=OP.mult)
            f_n = 0
            for hh in range(4):
                c0 = hh * 128
                op("vector", None, "tensor_tensor", ap_sub(wd, c0, 64),
                   ap_sub(wb, c0, 64), ap_sub(wc, c0, 64), op=OP.subtract)
                f_n = op("vector", sV if hh == 3 else None, "tensor_tensor",
                         ap_sub(wd, c0 + 64, 64),
                         ap_sub(wb, c0 + 64, 64), ap_sub(wc, c0 + 64, 64), op=OP.add)
            wait("vector", sV, f_n)
            op("vector", None, "tensor_scalar", wb, wd, 0.0, None, op0=OP.max)
            v_n = op("vector", sV, "tensor_scalar", wc, wd, 0.0, None, op0=OP.min)
            wait("scalar", sV, v_n)
            s_n = op("scalar", sS, "activation", wc, wc, AF.Exp)
            wait("vector", sS, s_n)
            return op("vector", sV, "tensor_tensor", wa, wc, wb, op=OP.add)

        wks_spill = {0: 0, 1: 0}

        def emit_tm_pass(w_dram, bias_dram, kind, spill_dram):
            wait("sync", sT, sT.n)
            wait("sync", sV, sV.n)
            wait("sync", sG, sG.n)
            bb_done = emit_bias_bcast(bias_dram)
            last_mm = {}
            cons_lm = {}
            for oh in range(4):
                if oh > 0:
                    wait("sync", sT, last_mm[(oh - 1, NT - 1)])
                ld = 0
                for k in range(KC):
                    ld = dma("sync", sL, wslot(k),
                             dram_ap(w_dram, k * 128 * D + oh * 512, [[D, 128], [1, 512]]))
                wait("tensor", sL, ld)
                wait("tensor", sV, ln1_done)
                for t in range(NT):
                    p = pb[t % 2]
                    use_bank(t % 2)
                    for k in range(KC):
                        last_mm[(oh, t)] = op(
                            "tensor", sT if k == KC - 1 else None, "matmul",
                            p[0:128, 0:512], a1(k * 1024 + t * 128, 128), wslot(k),
                            start=(k == 0), stop=(k == KC - 1))
                    tp = t % 2
                    wa, wb, wc, wd = (wks(tp, 0), wks(tp, 1), wks(tp, 2), wks(tp, 3))
                    wait("vector", sT, last_mm[(oh, t)])
                    wait("vector", sL, bb_done)
                    wait("vector", sG, wks_spill[tp])
                    op("vector", None, "tensor_tensor", wa, p[0:128, 0:512],
                       a3(oh * 512, 512), op=OP.add)
                    if kind == "v":
                        fin = op("vector", sV, "nop")
                    else:
                        fin = emit_rope_elu_tm(wa, wb, wc, wd, t)
                    rd_bank(t % 2, sV, fin)
                    cons_lm[(oh, t)] = fin
                    if kind in ("k", "v"):
                        wait("gpsimd", sV, fin)
                        wks_spill[tp] = dma(
                            "gpsimd", sG,
                            dram_ap(spill_dram, t * 128 * D + oh * 512, [[D, 128], [1, 512]]),
                            wa)
                    else:
                        for hh in range(4):
                            bank = 4 + hh % 2
                            wait("tensor", sV, fin)
                            use_bank(bank)
                            t_n = op("tensor", sT, "transpose", pb[bank][0:128, 0:128],
                                     ap_sub(wa, hh * 128, 128), idn[:])
                            wait("vector", sT, t_n)
                            v_n = op("vector", sV, "tensor_copy",
                                     ap_sub(wb, hh * 128, 128), pb[bank][0:128, 0:128])
                            rd_bank(bank, sV, v_n)
                        cons_lm[(oh, t)] = sV.n
                        rd_bank(t % 2, sV, cons_lm[(oh, t)])
                        wait("gpsimd", sV, cons_lm[(oh, t)])
                        wks_spill[tp] = dma(
                            "gpsimd", sG,
                            dram_ap(spill_dram,
                                    (oh * 4) * NT * 128 * 128 + t * 128 * 128,
                                    [[128, 128], [NT * 128 * 128, 4], [1, 128]]),
                            bass.AP(wb.tensor, wb.offset, [list(wb.ap[0]), [128, 4], [1, 128]]))
            return sG.n

        emit_tm_pass(wk, biases["k"], "k", k_sp)
        emit_tm_pass(wv, biases["v"], "v", v_sp)
        kv_spill_done = sG.n

        if stage < 5:
            wait("gpsimd", sG, kv_spill_done)
            dma("gpsimd", sG, dram_ap(dbg["k"], 0, [[1, NT * 128 * D]]),
                dram_ap(k_sp, 0, [[1, NT * 128 * D]]))
            dma("gpsimd", sG, dram_ap(dbg["v"], 0, [[1, NT * 128 * D]]),
                dram_ap(v_sp, 0, [[1, NT * 128 * D]]))
        if stage == 1:
            dma("gpsimd", sG, dram_ap(dbg["tabs"], 0, [[1024, 128], [1, 512]]), stm[:])
            dma("gpsimd", sG, dram_ap(dbg["tabs"], 512, [[1024, 128], [1, 512]]), ctm[:])

        def kr(tp, off=0, n=2048, p=128, pbase=0):
            return bass.AP(A2[:].tensor, 8192 + tp * 2048 + off + pbase * A2W,
                           [[A2W, p], [1, n]])

        def vr(tp, off=0, n=2064, p=128, pbase=0):
            return bass.AP(A2[:].tensor, 12288 + tp * 2064 + off + pbase * A2W,
                           [[A2W, p], [1, n]])

        wait("vector", sG, kv_spill_done)
        for tp in range(2):
            for h in range(H):
                op("vector", None, "memset", vr(tp, h * 129 + 128, 1), 1.0)
        op("vector", None, "memset", S_all[:], 0.0)
        vring_ones = op("vector", sV, "nop")

        p3_mm = {}
        p3_add = {}
        for tau in range(NT):
            tp = tau % 2
            if tau >= 2:
                wait("sync", sT, p3_mm[(tau - 2, H - 1)])
            wait("sync", sV, vring_ones)
            dma("sync", sL, kr(tp), dram_ap(k_sp, tau * 128 * D, [[D, 128], [1, D]]))
            ld = dma("sync", sL,
                     bass.AP(A2[:].tensor, 12288 + tp * 2064, [[A2W, 128], [129, 16], [1, 128]]),
                     dram_ap(v_sp, tau * 128 * D, [[D, 128], [128, 16], [1, 128]]))
            wait("tensor", sL, ld)
            for h in range(H):
                bi = 4 + (tau * H + h) % 2
                use_bank(bi)
                p3_mm[(tau, h)] = op("tensor", sT, "matmul", pb[bi][0:128, 0:129],
                                     kr(tp, h * 128, 128), vr(tp, h * 129, 129),
                                     start=True, stop=True)
                wait("vector", sT, p3_mm[(tau, h)])
                p3_add[tau * H + h] = op(
                    "vector", sV, "tensor_tensor", S_all[:, h * 129:(h + 1) * 129],
                    S_all[:, h * 129:(h + 1) * 129], pb[bi][0:128, 0:129], op=OP.add)
                rd_bank(bi, sV, p3_add[tau * H + h])
        p3_done = sV.n

        wait("gpsimd", sV, p3_done)
        dma("gpsimd", sG, dram_ap(cc_in.ap(), 0, [[129, 128], [129 * 128, 16], [1, 129]]),
            bass.AP(S_all[:].tensor, 0, [[H * 129, 128], [129, 16], [1, 129]]))
        wait("gpsimd", sG)

        def f_cc(e):
            e.collective_compute(
                "AllGather", OP.bypass,
                replica_groups=[list(range(NCORES))],
                ins=[cc_in.ap().opt()],
                outs=[cc_out.ap().opt()],
            ).then_inc(sem_h[id(sC)], 1)
        emit("gpsimd", f_cc)
        sC.n += 1

        if stage in (2, 3, 4):
            dma("gpsimd", sG, dram_ap(dbg["S"], 0, [[129, 128], [129 * 128, 16], [1, 129]]),
                bass.AP(S_all[:].tensor, 0, [[H * 129, 128], [129, 16], [1, 129]]))

        def emit_fm_pass(w_dram, bname, spill_dram, func):
            bc = {"q": 0, "k": 16, "v": 32, "g": 48, "o": 64, "f": 80}[bname]
            mm_lm = {}
            sp_lm = {}
            act_lm = {}
            wait("sync", sT, sT.n)
            wait("sync", sV, sV.n)
            for oc in range(H):
                sl = oc % 2
                if oc >= 2:
                    wait("sync", sT, mm_lm[(oc - 2, 1)])
                ld = 0
                for k in range(KC):
                    ld = dma("sync", sL, a3(sl * 2048 + k * 128, 128),
                             dram_ap(w_dram, k * 128 * D + oc * 128, [[D, 128], [1, 128]]))
                wait("tensor", sL, ld)
                wait("tensor", sV, ln1_done)
                for th in range(2):
                    p = pb[2 + th]
                    use_bank(2 + th)
                    for k in range(KC):
                        mm_lm[(oc, th)] = op(
                            "tensor", sT if k == KC - 1 else None, "matmul",
                            p[0:128, 0:512], a3(sl * 2048 + k * 128, 128),
                            a1(k * 1024 + th * 512, 512),
                            start=(k == 0), stop=(k == KC - 1))
                    wait("scalar", sT, mm_lm[(oc, th)])
                    if oc >= 1:
                        wait("scalar", sG, sp_lm.get((oc - 1, th), 0))
                    act_lm[(oc, th)] = op("scalar", sS, "activation", a2(th * 512, 512),
                                          p[0:128, 0:512], func,
                                          bias=bcol[:, bc + oc:bc + oc + 1])
                    rd_bank(2 + th, sS, act_lm[(oc, th)])
                    wait("gpsimd", sS, act_lm[(oc, th)])
                    sp_lm[(oc, th)] = dma(
                        "gpsimd", sG,
                        dram_ap(spill_dram, oc * NT * 128 * 128 + th * 4 * 128 * 128,
                                [[128, 128], [128 * 128, 4], [1, 128]]),
                        bass.AP(A2[:].tensor, th * 512, [[A2W, 128], [128, 4], [1, 128]]))

        emit_fm_pass(wg, "g", g_sp, AF.Sigmoid)
        emit_tm_pass(wq, biases["q"], "q", q_sp)
        qg_spill_done = sG.n
        q_tensor_done = sT.n

        if stage < 5:
            wait("gpsimd", sG, qg_spill_done)
            dma("gpsimd", sG, dram_ap(dbg["q"], 0, [[1, H * NT * 128 * 128]]),
                dram_ap(q_sp, 0, [[1, H * NT * 128 * 128]]))
            dma("gpsimd", sG, dram_ap(dbg["g"], 0, [[1, H * NT * 128 * 128]]),
                dram_ap(g_sp, 0, [[1, H * NT * 128 * 128]]))
        if stage == 1:
            wait("gpsimd", sC, 1)
            wait("gpsimd", sG)
            finish()
            return nc

        op("vector", None, "memset", S_sb[:], 0.0)
        wait("gpsimd", sC, 1)
        wait("gpsimd", sT, q_tensor_done)
        stmp = a1(0, 2064)
        cons_lm = 0
        for cp in range(NCORES):
            if cp >= 1:
                wait("gpsimd", sV, cons_lm)
            dma("gpsimd", sG,
                bass.AP(A1[:].tensor, 0, [[A1W, 128], [129, 16], [1, 129]]),
                dram_ap(cc_out.ap(), cp * D * 129, [[129, 128], [129 * 128, 16], [1, 129]]))
            g_lm = dma("gpsimd", sG, pmc[:],
                       bass.AP(pmask_d.tensor, cp, [[0, 128], [1, 1]]))
            wait("vector", sG, g_lm)
            cons_lm = op("vector", sV, "scalar_tensor_tensor", S_sb[:],
                         stmp, pmc[:], S_sb[:], op0=OP.mult, op1=OP.add)
        seed_done = cons_lm

        if stage == 2:
            wait("gpsimd", sV, seed_done)
            dma("gpsimd", sG, dram_ap(dbg["Ssb"], 0, [[129, 128], [129 * 128, 16], [1, 129]]),
                bass.AP(S_sb[:].tensor, 0, [[H * 129, 128], [129, 16], [1, 129]]))
        if stage == 2:
            wait("gpsimd", sG)
            finish()
            return nc

        def fwh(h):
            return dram_ap(fw, (h // 2) * 128 * D + (h % 2) * 1024, [[D, 128], [1, 1024]])

        def a4(off, n):
            return bass.AP(A4[:].tensor, off, [[2048, 128], [1, n]])

        def sall4(off, n=1024):
            return bass.AP(S_all[:].tensor, off, [[H * 129, 128], [1, n]])

        sc_ld = {}
        sc_lm = {}
        t_ld = {}
        t_g = {}
        th_dv = [0]
        pa_v, pb_v = pad_scratch["vector"]
        pa_s, pb_s = pad_scratch["scalar"]
        if stage >= 5:
            op("vector", None, "memset", accc[:], 0.0)
            for h in (0, 1):
                sc_ld[h] = dma("gpsimd", sG, a4((h % 2) * 1024, 1024), fwh(h))

        def ffn_scale_half(h):
            wait("vector", sG, sc_ld[h])
            op("vector", None, "tensor_reduce", rcol[:], a4((h % 2) * 1024, 1024),
               axis=AX.X, op=OP.add, apply_absolute_value=True)
            pad("vector")
            pad("vector")
            pad("vector")
            op("vector", None, "tensor_copy", rcol2[:], rcol[:])
            pad("vector")
            pad("vector")
            lm = op("vector", sV, "tensor_tensor", accc[:], accc[:], rcol2[:], op=OP.add)
            sc_lm[h] = lm
            if h + 2 < 32:
                wait("gpsimd", sV, lm)
                sc_ld[h + 2] = dma("gpsimd", sG, a4((h % 2) * 1024, 1024), fwh(h + 2))

        def ffn_threshold():
            pad("vector")
            pad("vector")
            acc_done = op("vector", sV, "tensor_copy", pb_v, pa_v)
            wait("sync", sV, acc_done)
            l1 = dma("sync", sL, dram_ap(bc_d, 5120, [[1, 1], [1, 128]]), accc[:])
            wait("sync", sL, l1)
            l2 = dma("sync", sL, trow[:], dram_ap(bc_d, 5120, [[1, 1], [1, 128]]))
            wait("vector", sL, l2)
            op("vector", None, "tensor_reduce", tot[:], trow[:], axis=AX.X, op=OP.add)
            pad("vector")
            pad("vector")
            tot_n = op("vector", sV, "tensor_copy", pb_v, pa_v)
            wait("scalar", sV, tot_n)
            op("scalar", None, "activation", t11a[:], tot[:], AF.Identity,
               bias=c_e11[:], scale=1.0 / (1 << 23))
            op("scalar", None, "activation", t11s[:], tot[:], AF.Identity,
               scale=1.0 / (1 << 22))
            op("scalar", None, "activation", t11n[:], t11a[:], AF.Identity, scale=-1.0)
            pad("scalar")
            pad("scalar")
            s_n = op("scalar", sS, "activation", pb_s, pa_s, AF.Identity)
            th = 0
            for src, dst, boff in ((t11a, thc, 5504), (t11n, nthc, 5760),
                                   (t11s, scc, 5888)):
                th = emit_row_bcast(src[:], dst[:], 1, boff, sS, s_n)
            return th

        def ffn_tern_half(h):
            wait("vector", sG, t_ld[h])
            wait("vector", sL, th_dv[0])
            if h >= 1:
                wait("vector", sG, t_g[h - 1])
            op("vector", None, "tensor_scalar", sall4(0),
               a4((h % 2) * 1024, 1024), thc[:], None, op0=OP.is_gt)
            op("vector", None, "tensor_scalar", sall4(1024),
               a4((h % 2) * 1024, 1024), nthc[:], None, op0=OP.is_lt)
            pad("vector")
            op("vector", None, "tensor_tensor", sall4(0), sall4(0),
               sall4(1024), op=OP.subtract)
            pad("vector")
            pad("vector")
            lm = op("vector", sV, "tensor_copy", pb_v, pa_v)
            wait("gpsimd", sV, lm)
            t_g[h] = dma("gpsimd", sG,
                         dram_ap(wt_sp, (h // 2) * 128 * D + (h % 2) * 1024,
                                 [[D, 128], [1, 1024]]),
                         sall4(0))
            if h + 2 < 32:
                t_ld[h + 2] = dma("gpsimd", sG, a4((h % 2) * 1024, 1024), fwh(h + 2))

        def ffn_prep(b):
            if stage < 5:
                return
            if b < 8:
                for j in range(4):
                    ffn_scale_half(4 * b + j)
                return
            if b == 8:
                th_dv[0] = ffn_threshold()
                wait("gpsimd", sV, sc_lm[31])
                for h in (0, 1):
                    t_ld[h] = dma("gpsimd", sG, a4((h % 2) * 1024, 1024), fwh(h))
            for j in range(4):
                ffn_tern_half(4 * (b - 8) + j)

        def qr(tp, off=0, n=128, p=128, pbase=0):
            return bass.AP(A2[:].tensor, tp * 2048 + off + pbase * A2W,
                           [[A2W, p], [1, n]])

        def gr(tp, off=0, n=64):
            return bass.AP(A2[:].tensor, 4096 + tp * 2048 + off, [[A2W, 128], [1, n]])

        gate_lm = {}
        add_lm = {}
        att_lm = {}
        scan_ld = [0] * NT
        blk_gate = sV.n
        blk_tr = 0
        for tau in range(NT):
            tp = tau % 2
            if tau >= 2:
                wait("sync", sV, gate_lm[(tau - 2, 1)])
            else:
                wait("sync", sV, seed_done)
            dma("sync", sL, bass.AP(A2[:].tensor, tp * 2048, [[A2W, 128], [128, 16], [1, 128]]),
                dram_ap(q_sp, tau * 128 * 128, [[128, 128], [NT * 128 * 128, 16], [1, 128]]))
            dma("sync", sL, bass.AP(A2[:].tensor, 4096 + tp * 2048,
                                    [[A2W, 128], [128, 16], [1, 128]]),
                dram_ap(g_sp, tau * 128 * 128, [[128, 128], [NT * 128 * 128, 16], [1, 128]]))
            dma("sync", sL, kr(tp), dram_ap(k_sp, tau * 128 * D, [[D, 128], [1, D]]))
            scan_ld[tau] = dma(
                "sync", sL,
                bass.AP(A2[:].tensor, 12288 + tp * 2064, [[A2W, 128], [129, 16], [1, 128]]),
                dram_ap(v_sp, tau * 128 * D, [[D, 128], [128, 16], [1, 128]]))
            wait("tensor", sL, scan_ld[tau])
            for ic in range(2):
                add_half = blk_gate
                for half in range(2):
                    wait("tensor", sV, add_half)
                    ds_last = 0
                    for b in range(8):
                        h = half * 8 + b
                        ds_last = op("tensor", sT if b == 7 else None, "matmul",
                                     pb[b][0:128, 0:129],
                                     kr(tp, h * 128, 128, p=64, pbase=ic * 64),
                                     vr(tp, h * 129, 129, p=64, pbase=ic * 64),
                                     start=True, stop=True)
                    wait("vector", sT, ds_last)
                    for b in range(8):
                        h = half * 8 + b
                        add_half = op("vector", sV if b == 7 else None, "tensor_tensor",
                                      S_sb[:, h * 129:(h + 1) * 129],
                                      S_sb[:, h * 129:(h + 1) * 129],
                                      pb[b][0:128, 0:129], op=OP.add)
                pad("vector")
                pad("vector")
                pa, pb2 = pad_scratch["vector"]
                add_last = op("vector", sV, "tensor_copy", pb2, pa)
                ev_s = 0
                for half in range(2):
                    wait("tensor", sV, add_last)
                    if half == 1:
                        wait("tensor", sS, ev_s)
                    nm_last = 0
                    for b in range(8):
                        h = half * 8 + b
                        nm_last = op("tensor", sT if b == 7 else None, "matmul",
                                     pb[b][0:64, 0:129],
                                     qr(tp, h * 128 + ic * 64, 64),
                                     S_sb[:, h * 129:(h + 1) * 129],
                                     start=True, stop=True)
                    wait("scalar", sT, nm_last)
                    for b in range(8):
                        h = half * 8 + b
                        ev_s = op("scalar", sS if b == 7 else None, "activation",
                                  bass.AP(A3[:].tensor, h * 129, [[A3W, 64], [1, 129]]),
                                  pb[b][0:64, 0:129], AF.Identity)
                wait("vector", sS, ev_s)
                pad("vector")
                op("vector", None, "tensor_scalar", dnall[:],
                   bass.AP(A3[:].tensor, 128, [[A3W, 64], [129, 16]]),
                   1e-6, None, op0=OP.add)
                pad("vector")
                op("vector", None, "reciprocal", rcall[:], dnall[:])
                pad("vector")
                pad("vector")
                rc_lm = op("vector", sV, "tensor_copy", pb2, pa)
                wait("scalar", sV, rc_lm)
                att_last = 0
                for h in range(H):
                    att_last = op("scalar", sS if h == H - 1 else None, "activation",
                                  bass.AP(A3[:].tensor, 2064 + h * 128, [[A3W, 64], [1, 128]]),
                                  bass.AP(A3[:].tensor, h * 129, [[A3W, 64], [1, 128]]),
                                  AF.Identity,
                                  scale=bass.AP(rcall[:].tensor, h, [[16, 64], [1, 1]]))
                att_lm[(tau, ic)] = att_last
                if stage in (3, 4) and tau == 0 and ic == 0:
                    wait("vector", sG, s_cp)
                    op("vector", None, "tensor_copy", dbg_h0_sb[:], S_sb[0:128, 0:256])
                    op("vector", None, "tensor_copy",
                       bass.AP(dbg_num_sb[:].tensor, 0, [[516, 64], [1, 129]]),
                       bass.AP(A3[:].tensor, 0, [[A3W, 64], [1, 129]]))
                    op("vector", None, "tensor_copy",
                       bass.AP(dbg_num_sb[:].tensor, 129, [[516, 64], [1, 16]]),
                       dnall[:])
                    op("vector", None, "tensor_copy",
                       bass.AP(dbg_num_sb[:].tensor, 145, [[516, 64], [1, 16]]),
                       rcall[:])
                    op("vector", None, "tensor_copy",
                       bass.AP(dbg_num_sb[:].tensor, 161, [[516, 64], [1, 128]]),
                       bass.AP(A3[:].tensor, 2064, [[A3W, 64], [1, 128]]))
                g_last = 0
                for half in range(2):
                    if half == 0:
                        wait("tensor", sS, att_last)
                    else:
                        wait("tensor", sV, g_last)
                    tr_last = 0
                    for b in range(8):
                        h = half * 8 + b
                        tr_last = op("tensor", sT if b == 7 else None, "transpose",
                                     pb[b][0:128, 0:64],
                                     bass.AP(A3[:].tensor, 2064 + h * 128,
                                             [[A3W, 64], [1, 128]]),
                                     idn[0:64, 0:64])
                    wait("vector", sT, tr_last)
                    for b in range(8):
                        h = half * 8 + b
                        g_last = op("vector", sV if b == 7 else None, "tensor_tensor",
                                    a1(h * 1024 + tau * 128 + ic * 64, 64),
                                    pb[b][0:128, 0:64],
                                    gr(tp, h * 128 + ic * 64, 64),
                                    op=OP.mult)
                blk_gate = g_last
                gate_lm[(tau, ic)] = g_last
                ffn_prep(tau * 2 + ic)
        scan_done = sV.n

        if stage in (3, 4):
            wait("gpsimd", sV, scan_done)
            dma("gpsimd", sG, dram_ap(dbg["num"], 0, [[129 * 4, 64], [1, 129 * 4]]),
                bass.AP(dbg_num_sb[:].tensor, 0, [[129 * 4, 64], [1, 129 * 4]]))
            dma("gpsimd", sG, dram_ap(dbg["h0"], 0, [[256, 128], [1, 256]]),
                dbg_h0_sb[:])
            dma("gpsimd", sG, dram_ap(dbg["Ssb"], 0, [[129, 128], [129 * 128, 16], [1, 129]]),
                bass.AP(S_sb[:].tensor, 0, [[H * 129, 128], [129, 16], [1, 129]]))
            for k in range(KC):
                dma("gpsimd", sG, dram_ap(dbg["attn"], k * 128 * T, [[T, 128], [1, T]]),
                    a1(k * 1024, 1024))
        if stage == 3:
            wait("gpsimd", sG)
            finish()
            return nc

        wo_mm = {}
        wo_fin = {}
        wait("sync", sV, scan_done)
        for oc in range(H):
            sl = oc % 2
            if oc >= 2:
                wait("sync", sT, wo_mm[(oc - 2, 1)])
            if oc >= 1:
                wait("sync", sV, wo_fin.get((oc - 1) * 2 + 1, 0))
            for k in range(KC):
                dma("sync", sL, a3(sl * 2048 + k * 128, 128),
                    dram_ap(wo, k * 128 * D + oc * 128, [[D, 128], [1, 128]]))
            for th in range(2):
                dma("sync", sL, a3(4096 + th * 512, 512),
                    dram_ap(x_t, oc * 128 * T + th * 512, [[T, 128], [1, 512]]))
            ld = sL.n
            wait("tensor", sL, ld)
            wait("tensor", sV, scan_done)
            for th in range(2):
                bi = (oc * 2 + th) % 4
                use_bank(bi)
                for k in range(KC):
                    wo_mm[(oc, th)] = op(
                        "tensor", sT if k == KC - 1 else None, "matmul",
                        pb[bi][0:128, 0:512], a3(sl * 2048 + k * 128, 128),
                        a1(k * 1024 + th * 512, 512),
                        start=(k == 0), stop=(k == KC - 1))
                wait("scalar", sT, wo_mm[(oc, th)])
                if oc >= 1:
                    wait("scalar", sV, wo_fin.get((oc - 1) * 2 + th, 0))
                s_n = op("scalar", sS, "activation", a3(5120 + th * 512, 512),
                         pb[bi][0:128, 0:512], AF.Identity, bias=bcol[:, 64 + oc:64 + oc + 1])
                rd_bank(bi, sS, s_n)
                wait("vector", sS, s_n)
                wait("vector", sL, ld)
                wo_fin[oc * 2 + th] = op(
                    "vector", sV, "tensor_tensor", a2(oc * 1024 + th * 512, 512),
                    a3(5120 + th * 512, 512), a3(4096 + th * 512, 512), op=OP.add)
        x2_done = sV.n

        if stage == 4:
            wait("gpsimd", sV, x2_done)
            for k in range(KC):
                dma("gpsimd", sG, dram_ap(dbg["x2"], k * 128 * T, [[T, 128], [1, T]]),
                    a2(k * 1024, 1024))
            wait("gpsimd", sG)
            finish()
            return nc

        wait("tensor", sV, x2_done)
        wait("scalar", sV, x2_done)
        ln2_done = emit_ln(
            lambda k: a2(k * 1024, 1024), lambda k: a1(k * 1024, 1024),
            None,
            [a1(0, 1024), a1(1024, 1024)],
            lambda off, n=512: a3row(2048 + off, n),
            [a3(0, 1024), a3(1024, 1024)],
            False, 3072,
        )

        slot_mm = {0: 0, 1: 0}
        ff_fin = {}
        st_lm = {0: 0, 1: 0}
        wait("sync", sG, t_g[31])
        wait("sync", sV, ln2_done)
        wait("tensor", sV, ln2_done)
        for pi in range(4):
            th, og = pi // 2, pi % 2
            for k in range(KC):
                slw = k % 2
                wait("sync", sT, slot_mm[slw])
                ld = dma("sync", sL, a3(slw * 1024, 1024),
                         dram_ap(wt_sp, k * 128 * D + og * 1024, [[D, 128], [1, 1024]]))
                wait("tensor", sL, ld)
                for oc in range(8):
                    if k == 0:
                        use_bank(oc)
                    slot_mm[slw] = op(
                        "tensor", sT if oc == 7 else None, "matmul",
                        pb[oc][0:128, 0:512], a3(slw * 1024 + oc * 128, 128),
                        a1(k * 1024 + th * 512, 512),
                        start=(k == 0), stop=(k == KC - 1))
            for oc in range(8):
                o_g = og * 8 + oc
                ft = bass.AP(ftmp[:].tensor, (oc % 2) * 512, [[1024, 128], [1, 512]])
                wait("scalar", sT, slot_mm[(KC - 1) % 2])
                wait("scalar", sL, th_dv[0])
                if pi > 0 or oc >= 2:
                    wait("scalar", sG, st_lm[oc % 2])
                s_n = op("scalar", sS, "activation", ft, pb[oc][0:128, 0:512],
                         AF.Identity, bias=bcol[:, 80 + o_g:80 + o_g + 1],
                         scale=scc[:])
                rd_bank(oc, sS, s_n)
                wait("vector", sS, s_n)
                op("vector", None, "tensor_tensor", ft, ft,
                   a2(o_g * 1024 + th * 512, 512), op=OP.add)
                pad("vector")
                pad("vector")
                ff_fin[(pi, oc)] = op("vector", sV, "tensor_copy", pb_v, pa_v)
                wait("gpsimd", sV, ff_fin[(pi, oc)])
                st_lm[oc % 2] = dma("gpsimd", sG,
                                    dram_ap(y_t, o_g * 128 * T + th * 512, [[T, 128], [1, 512]]),
                                    ft)

        wait("gpsimd", sG)
        finish()
    return nc


_ROPE_PERM = None


def _rope_perm():
    global _ROPE_PERM
    if _ROPE_PERM is None:
        p = []
        for h in range(H):
            base = h * DK
            p.extend(base + np.arange(0, DK, 2))
            p.extend(base + np.arange(1, DK, 2))
        _ROPE_PERM = np.asarray(p)
    return _ROPE_PERM


def _host_prep(inputs):
    perm = _rope_perm()
    f32 = np.float32
    x = np.asarray(inputs["L0_attn_x"], f32)
    wqT = np.ascontiguousarray(np.asarray(inputs["L0_attn_Wq"], f32).T[:, perm])
    wkT = np.ascontiguousarray(np.asarray(inputs["L0_attn_Wk"], f32).T[:, perm])
    wvT = np.ascontiguousarray(np.asarray(inputs["L0_attn_Wv"], f32).T)
    wgT = np.ascontiguousarray(np.asarray(inputs["L0_attn_Wg"], f32).T)
    woT = np.ascontiguousarray(np.asarray(inputs["L0_attn_Wo"], f32).T)
    fwT = np.ascontiguousarray(np.asarray(inputs["L0_ffn_weight"], f32).T)
    bq = np.ascontiguousarray(np.asarray(inputs["L0_attn_bq"], f32)[perm])
    bk = np.ascontiguousarray(np.asarray(inputs["L0_attn_bk"], f32)[perm])
    ident = np.eye(128, dtype=f32)
    xT = np.ascontiguousarray(x[0].T)
    in_maps = []
    for c in range(NCORES):
        in_maps.append(dict(
            x_t=np.ascontiguousarray(xT[:, c * T:(c + 1) * T]),
            wq=wqT, wk=wkT, wv=wvT, wg=wgT, wo=woT, fw=fwT,
            bq=bq, bk=bk,
            bv=np.asarray(inputs["L0_attn_bv"], f32),
            bg=np.asarray(inputs["L0_attn_bg"], f32),
            bo=np.asarray(inputs["L0_attn_bo"], f32),
            bf=np.asarray(inputs["L0_ffn_bias"], f32),
            pos=np.arange(c * T, (c + 1) * T, dtype=f32).reshape(1, T),
            jrow=np.arange(64, dtype=f32).reshape(1, 64),
            ident=ident,
            pmask=(np.arange(NCORES) < c).astype(f32).reshape(1, NCORES),
        ))
    return in_maps


def kernel(**inputs):
    in_maps = _host_prep(inputs)
    nc = _build(stage=5)
    res = run_bass_kernel_spmd(nc, in_maps, core_ids=list(range(NCORES)))
    out = np.empty((1, NCORES * T, D), np.float32)
    for c in range(NCORES):
        out[0, c * T:(c + 1) * T, :] = res.results[c]["y_t"].T
    return out


# version 12 (final)
# speedup vs baseline: 1.1427x; 1.1427x over previous
import numpy as np
from contextlib import ExitStack

import concourse.bass as bass
import concourse.mybir as mybir
from concourse.bass_utils import run_bass_kernel_spmd

F32 = mybir.dt.float32
I32 = mybir.dt.int32
AF = mybir.ActivationFunctionType
OP = mybir.AluOpType
AX = mybir.AxisListType

NCORES = 8
D = 2048
T = 1024
H = 16
DK = 128
CS = 64
NCH = T // CS
NT = T // 128
KC = D // 128
A1W, A2W, A3W = 16384, 16512, 6144


def _split3(v):
    a = np.float32(v)
    b = np.float32(v - float(a))
    c = np.float32(v - float(a) - float(b))
    return float(a), float(b), float(c)


TWO_PI = 2 * np.pi
C1 = 6.28125
C2 = float(np.float32(TWO_PI - C1))
C3 = float(np.float32(TWO_PI - C1 - C2))
INV_2PI = float(np.float32(1.0 / TWO_PI))
L1, L2, L3 = _split3(np.log(10000.0) / 64.0)
HPI1, HPI2, _ = _split3(np.pi / 2)


class _Ctr:
    def __init__(self):
        self.n = 0


def _build(stage=5):
    nc = bass.Bass("TRN2", target_bir_lowering=False, detect_race_conditions=False)

    x_t = nc.dram_tensor("x_t", [D, T], F32, kind="ExternalInput").ap()
    wq = nc.dram_tensor("wq", [D, D], F32, kind="ExternalInput").ap()
    wk = nc.dram_tensor("wk", [D, D], F32, kind="ExternalInput").ap()
    wv = nc.dram_tensor("wv", [D, D], F32, kind="ExternalInput").ap()
    wg = nc.dram_tensor("wg", [D, D], F32, kind="ExternalInput").ap()
    wo = nc.dram_tensor("wo", [D, D], F32, kind="ExternalInput").ap()
    fw = nc.dram_tensor("fw", [D, D], F32, kind="ExternalInput").ap()
    biases = {
        nm: nc.dram_tensor("b" + nm, [D], F32, kind="ExternalInput").ap()
        for nm in ("q", "k", "v", "g", "o", "f")
    }
    pos_d = nc.dram_tensor("pos", [1, T], F32, kind="ExternalInput").ap()
    jrow_d = nc.dram_tensor("jrow", [1, 64], F32, kind="ExternalInput").ap()
    ident_d = nc.dram_tensor("ident", [128, 128], F32, kind="ExternalInput").ap()
    pmask_d = nc.dram_tensor("pmask", [1, NCORES], F32, kind="ExternalInput").ap()

    y_t = nc.dram_tensor("y_t", [D, T], F32, kind="ExternalOutput").ap()

    q_sp = nc.dram_tensor("q_sp", [H * NT * 128, 128], F32).ap()
    g_sp = nc.dram_tensor("g_sp", [H * NT * 128, 128], F32).ap()
    k_sp = nc.dram_tensor("k_sp", [NT * 128, D], F32).ap()
    v_sp = nc.dram_tensor("v_sp", [NT * 128, D], F32).ap()
    wt_sp = nc.dram_tensor("wt_sp", [D, D], F32).ap()
    bc_d = nc.dram_tensor("bc_d", [1, 8192], F32).ap()
    cc_in = nc.dram_tensor("cc_in", [D, 129], F32)
    cc_out = nc.dram_tensor("cc_out", [NCORES * D, 129], F32)

    dbg = {}
    if stage < 5:
        dbg["tabs"] = nc.dram_tensor("dbg_tabs", [128, 1024], F32, kind="ExternalOutput").ap()
        dbg["xh"] = nc.dram_tensor("dbg_xh", [D, T], F32, kind="ExternalOutput").ap()
        dbg["k"] = nc.dram_tensor("dbg_k", [NT * 128, D], F32, kind="ExternalOutput").ap()
        dbg["v"] = nc.dram_tensor("dbg_v", [NT * 128, D], F32, kind="ExternalOutput").ap()
        dbg["q"] = nc.dram_tensor("dbg_q", [H * NT * 128, 128], F32, kind="ExternalOutput").ap()
        dbg["g"] = nc.dram_tensor("dbg_g", [H * NT * 128, 128], F32, kind="ExternalOutput").ap()
    if stage in (2, 3, 4):
        dbg["S"] = nc.dram_tensor("dbg_S", [D, 129], F32, kind="ExternalOutput").ap()
        dbg["Ssb"] = nc.dram_tensor("dbg_Ssb", [D, 129], F32, kind="ExternalOutput").ap()
    if stage in (3, 4):
        dbg["attn"] = nc.dram_tensor("dbg_attn", [D, T], F32, kind="ExternalOutput").ap()
        dbg["num"] = nc.dram_tensor("dbg_num", [64, 129 * 4], F32, kind="ExternalOutput").ap()
        dbg["h0"] = nc.dram_tensor("dbg_h0", [128, 256], F32, kind="ExternalOutput").ap()
    if stage == 4:
        dbg["x2"] = nc.dram_tensor("dbg_x2", [D, T], F32, kind="ExternalOutput").ap()
    if stage == 6:
        dbg["ffn"] = nc.dram_tensor("dbg_ffn", [128, 8], F32, kind="ExternalOutput").ap()

    prog = {e: [] for e in ("sync", "tensor", "vector", "scalar", "gpsimd")}
    sL, sG, sT, sV, sS, sC = (_Ctr() for _ in range(6))
    sem_h = {}
    waited = {}

    def emit(eng, fn):
        prog[eng].append(fn)

    def wait(eng, ctr, val=None):
        v = ctr.n if val is None else val
        if v <= 0:
            return
        key = (eng, id(ctr))
        if waited.get(key, -1) >= v:
            return
        waited[key] = v
        emit(eng, lambda e, v=v, c=ctr: e.wait_ge(sem_h[id(c)], v))

    def dma(eng, ctr, out, in_, slow=False):
        ctr.n += 16
        emit(eng, lambda e, out=out, in_=in_, c=ctr, s=slow:
             e.dma_start(out=out, in_=in_, allow_slow_non_contiguous=s)
             .then_inc(sem_h[id(c)], 16))
        return ctr.n

    fence_scratch = {}

    def op(eng, ctr, meth, *a, **kw):
        fence = kw.pop("fence", True)
        if ctr is None:
            emit(eng, lambda e, meth=meth, a=a, kw=kw: getattr(e, meth)(*a, **kw))
            return None
        ctr.n += 1
        if not fence or eng in ("sync", "tensor"):
            def f(e, meth=meth, a=a, kw=kw, c=ctr):
                getattr(e, meth)(*a, **kw).then_inc(sem_h[id(c)], 1)
            emit(eng, f)
            return ctr.n
        emit(eng, lambda e, meth=meth, a=a, kw=kw: getattr(e, meth)(*a, **kw))
        fa, fb = fence_scratch[eng]

        def g(e, c=ctr, eng=eng, fa=fa, fb=fb):
            if eng == "scalar":
                ins = e.activation(fb, fa, AF.Identity)
            else:
                ins = e.tensor_copy(fb, fa)
            ins.then_inc(sem_h[id(c)], 1)
        emit(eng, g)
        return ctr.n

    bank_rd = {}

    def use_bank(b):
        g = bank_rd.get(b)
        if g:
            wait("tensor", g[0], g[1])

    def rd_bank(b, ctr, val):
        bank_rd[b] = (ctr, val)

    def pad(eng):
        fa, fb = pad_scratch[eng]
        if eng == "scalar":
            emit(eng, lambda e, fa=fa, fb=fb: e.activation(fb, fa, AF.Identity))
        else:
            emit(eng, lambda e, fa=fa, fb=fb: e.tensor_copy(fb, fa))

    with ExitStack() as ctx:
        def sbuf(name, shape, dt=F32):
            return ctx.enter_context(nc.sbuf_tensor(name, shape, dt))

        A1 = sbuf("A1", [128, A1W])
        A2 = sbuf("A2", [128, A2W])
        A3 = sbuf("A3", [128, A3W])
        kint = sbuf("kint", [128, 64], I32)
        stm = sbuf("stm", [128, 512])
        ctm = sbuf("ctm", [128, 512])
        S_all = sbuf("S_all", [128, H * 129])
        S_sb = sbuf("S_sb", [128, H * 129])
        A4 = sbuf("A4", [128, 2048])
        t11s = sbuf("t11s", [1, 1])
        nscc = sbuf("nscc", [128, 1])
        bcol = sbuf("bcol", [128, 96])
        irow = sbuf("irow", [1, 64])
        s_e = sbuf("s_e", [1, 64])
        ones_r = sbuf("ones_r", [1, 128])
        ones_c = sbuf("ones_c", [128, 1])
        idn = sbuf("idn", [128, 128])
        posc = sbuf("posc", [128, NT])
        ivfb = sbuf("ivfb", [128, 64])
        atm = sbuf("atm", [128, 64])
        atmc = sbuf("atmc", [128, 64])
        mtm = sbuf("mtm", [128, 64])
        mtk = sbuf("mtk", [128, 64])
        mta = sbuf("mta", [128, 64])
        accc = sbuf("accc", [128, 1])
        rcol = sbuf("rcol", [128, 1])
        rcol2 = sbuf("rcol2", [128, 1])
        thc = sbuf("thc", [128, 1])
        nthc = sbuf("nthc", [128, 1])
        scc = sbuf("scc", [128, 1])
        t11a = sbuf("t11a", [1, 1])
        t11b = sbuf("t11b", [1, 1])
        t11n = sbuf("t11n", [1, 1])
        tot = sbuf("tot", [1, 1])
        trow = sbuf("trow", [1, 128])
        pmc = sbuf("pmc", [128, 1])
        pmc8 = sbuf("pmc8", [128, NCORES])
        c_eps = sbuf("c_eps", [1, 1])
        c_e11 = sbuf("c_e11", [1, 1])
        dnall = sbuf("dnall", [64, 16])
        rcall = sbuf("rcall", [64, 16])
        dnc = sbuf("dnc", [64, 1])
        rcc = sbuf("rcc", [64, 1])
        ftmp = sbuf("ftmp", [128, 1024])
        if stage in (3, 4):
            dbg_num_sb = sbuf("dbg_num_sb", [64, 129 * 4])
            dbg_h0_sb = sbuf("dbg_h0_sb", [128, 256])
        if stage in (3, 4):
            pass
        pad_scratch = {}
        for _e in ("vector", "scalar", "gpsimd"):
            _fa = sbuf("fence_a_" + _e, [1, 1])
            _fb = sbuf("fence_b_" + _e, [1, 1])
            fence_scratch[_e] = (_fa[:], _fb[:])
            _pa = sbuf("pad_a_" + _e, [64, 64])
            _pb2 = sbuf("pad_b_" + _e, [64, 64])
            pad_scratch[_e] = (_pa[:], _pb2[:])

        pb = [ctx.enter_context(nc.psum_tensor("pb%d" % i, [128, 512], F32))
              for i in range(8)]

        for name, c in (("semL", sL), ("semG", sG), ("semT", sT),
                        ("semV", sV), ("semS", sS), ("semC", sC)):
            sem_h[id(c)] = ctx.enter_context(nc.semaphore(name))

        def a1(off, n):
            return bass.AP(A1[:].tensor, off, [[A1W, 128], [1, n]])

        def a2(off, n, p=128, pbase=0):
            return bass.AP(A2[:].tensor, off + pbase * A2W, [[A2W, p], [1, n]])

        def a3(off, n):
            return bass.AP(A3[:].tensor, off, [[A3W, 128], [1, n]])

        def a3row(off, n=512):
            return bass.AP(A3[:].tensor, off, [[A3W, 1], [1, n]])

        def ap_sub(ap, doff, n):
            return bass.AP(ap.tensor, ap.offset + doff, [list(ap.ap[0]), [1, n]])

        def dram_ap(d, offset, dims):
            return bass.AP(d.tensor, offset, dims)

        def finish():
            with nc.Block() as blk:
                for name, fns in prog.items():
                    def run_all(e, fns=fns):
                        for f in fns:
                            f(e)
                    getattr(blk, name)(run_all)

        dma("sync", sL, idn[:], ident_d)
        for i, nm in enumerate(("q", "k", "v", "g", "o", "f")):
            dma("sync", sL, bass.AP(bcol[:].tensor, i * 16, [[96, 128], [1, 16]]),
                dram_ap(biases[nm], 0, [[1, 128], [128, 16]]), slow=True)
        dma("sync", sL, posc[:], dram_ap(pos_d, 0, [[1, 128], [128, NT]]), slow=True)
        p0_loads = sL.n

        op("vector", None, "memset", ones_r[:], 1.0)
        op("vector", None, "memset", ones_c[:], 1.0)
        op("vector", None, "memset", c_eps[:], 1e-5)
        op("vector", None, "memset", c_e11[:], 5e-11)
        for _e in ("vector", "scalar", "gpsimd"):
            op("vector", None, "memset", fence_scratch[_e][0], 0.0)
            op("vector", None, "memset", pad_scratch[_e][0], 0.0)
        if stage in (3, 4):
            op("vector", None, "memset", dbg_h0_sb[:], 0.0)
            op("vector", None, "memset", dbg_num_sb[:], 0.0)
        jld = dma("sync", sL, s_e[:], jrow_d)
        wait("vector", sL, jld)
        op("vector", None, "tensor_scalar", irow[:], s_e[:], -L1, None, op0=OP.mult)
        op("vector", None, "scalar_tensor_tensor", irow[:], s_e[:], -L2, irow[:],
           op0=OP.mult, op1=OP.add)
        v_n = op("vector", sV, "scalar_tensor_tensor", irow[:], s_e[:], -L3, irow[:],
                 op0=OP.mult, op1=OP.add)
        wait("scalar", sV, v_n)
        s_n = op("scalar", sS, "activation", irow[:], irow[:], AF.Exp)
        ivf_done = s_n

        def emit_row_bcast(row_ap, out_ap, n, boff, prod_ctr, prod_val):
            wait("sync", prod_ctr, prod_val)
            l1 = dma("sync", sL, dram_ap(bc_d, boff, [[1, 1], [1, n]]), row_ap)
            wait("sync", sL, l1)
            return dma("sync", sL, out_ap,
                       bass.AP(bc_d.tensor, boff, [[0, out_ap.ap[0][1]], [1, n]]))

        ivfb_ld = emit_row_bcast(irow[:], ivfb[:], 64, 0, sS, ivf_done)
        wait("vector", sL, ivfb_ld)

        wait("vector", sL, p0_loads)
        for tau in range(NT):
            wait("vector", sS, sS.n)
            op("vector", None, "tensor_scalar", atm[:], ivfb[:], posc[:, tau:tau + 1],
               None, op0=OP.mult)
            op("vector", None, "tensor_scalar", atmc[:], atm[:], HPI1, HPI2,
               op0=OP.add, op1=OP.add)
            for src, dstt in ((atm, stm), (atmc, ctm)):
                op("vector", None, "tensor_scalar", mtm[:], src[:], INV_2PI, None, op0=OP.mult)
                op("vector", None, "tensor_copy", kint[:], mtm[:])
                op("vector", None, "tensor_copy", mtk[:], kint[:])
                op("vector", None, "tensor_tensor", mtm[:], mtm[:], mtk[:], op=OP.subtract)
                op("vector", None, "tensor_scalar", mta[:], mtm[:], 0.499999, None, op0=OP.is_gt)
                op("vector", None, "tensor_tensor", mtk[:], mtk[:], mta[:], op=OP.add)
                op("vector", None, "tensor_scalar", mta[:], mtm[:], -0.499999, None, op0=OP.is_lt)
                op("vector", None, "tensor_tensor", mtk[:], mtk[:], mta[:], op=OP.subtract)
                op("vector", None, "scalar_tensor_tensor", src[:], mtk[:], -C1, src[:],
                   op0=OP.mult, op1=OP.add)
                op("vector", None, "scalar_tensor_tensor", src[:], mtk[:], -C2, src[:],
                   op0=OP.mult, op1=OP.add)
                op("vector", None, "scalar_tensor_tensor", src[:], mtk[:], -C3, src[:],
                   op0=OP.mult, op1=OP.add)
                v_n = op("vector", sV, "tensor_scalar", src[:], src[:], 3.141592,
                         -3.141592, op0=OP.min, op1=OP.max)
                wait("scalar", sV, v_n)
                op("scalar", sS, "activation", dstt[:, tau * 64:(tau + 1) * 64], src[:], AF.Sin)
        tables_done = sS.n
        if stage in (-1, 0):
            wait("gpsimd", sS, tables_done)
            dma("gpsimd", sG, dram_ap(dbg["tabs"], 0, [[1024, 128], [1, 512]]), stm[:])
            dma("gpsimd", sG, dram_ap(dbg["tabs"], 512, [[1024, 128], [1, 512]]), ctm[:])
        if stage == -1:
            wait("gpsimd", sG)
            finish()
            return nc

        def emit_ln(src_tile_fn, dst_off_fn, xbuf, sq, srow, ab, from_dram, bc_base):
            mm_lm = [0] * KC
            sq_lm = [0] * KC
            for k in range(KC):
                if from_dram:
                    if k >= 2:
                        wait("sync", sT, mm_lm[k - 2])
                    ld = dma("sync", sL, xbuf[k % 2], src_tile_fn(k))
                src_ap = xbuf[k % 2] if from_dram else src_tile_fn(k)
                if k >= 2:
                    wait("scalar", sT, mm_lm[k - 2])
                if from_dram:
                    wait("scalar", sL, ld)
                sq_lm[k] = op("scalar", sS, "activation", sq[k % 2], src_ap, AF.Square)
                wait("tensor", sS, sq_lm[k])
                if k == 0:
                    for b in range(4):
                        use_bank(b)
                for th in range(2):
                    op("tensor", None, "matmul", pb[th][0:1, 0:512], ones_c[:],
                       ap_sub(src_ap, th * 512, 512), start=(k == 0), stop=(k == KC - 1))
                op("tensor", None, "matmul", pb[2][0:1, 0:512], ones_c[:],
                   ap_sub(sq[k % 2], 0, 512), start=(k == 0), stop=(k == KC - 1))
                mm_lm[k] = op("tensor", sT, "matmul", pb[3][0:1, 0:512], ones_c[:],
                              ap_sub(sq[k % 2], 512, 512), start=(k == 0), stop=(k == KC - 1))
            wait("vector", sT, mm_lm[KC - 1])
            for th in range(2):
                op("vector", None, "tensor_scalar", srow(th * 512, 512), pb[th][0:1, 0:512],
                   1.0 / D, None, op0=OP.mult)
                op("vector", None, "tensor_tensor", srow(1024 + th * 512, 512),
                   srow(th * 512, 512), srow(th * 512, 512), op=OP.mult)
                v_n = op("vector", sV, "scalar_tensor_tensor", srow(2048 + th * 512, 512),
                         pb[2 + th][0:1, 0:512], 1.0 / D, srow(1024 + th * 512, 512),
                         op0=OP.mult, op1=OP.subtract)
                rd_bank(th, sV, v_n)
                rd_bank(2 + th, sV, v_n)
                wait("scalar", sV, v_n)
                s_n = op("scalar", sS, "activation", srow(1024 + th * 512, 512),
                         srow(2048 + th * 512, 512), AF.Sqrt, bias=c_eps[:])
                wait("vector", sS, s_n)
                op("vector", None, "reciprocal", srow(2048 + th * 512, 512),
                   srow(1024 + th * 512, 512))
                op("vector", sV, "scalar_tensor_tensor", srow(3072 + th * 512, 512),
                   srow(th * 512, 512), -1.0, srow(2048 + th * 512, 512),
                   op0=OP.mult, op1=OP.mult)
            ab_lds = []
            for i, soff in enumerate((2048, 3072)):
                for th in range(2):
                    ab_lds.append(emit_row_bcast(
                        srow(soff + th * 512, 512), ap_sub(ab[i], th * 512, 512),
                        512, bc_base + (i * 2 + th) * 512, sV, sV.n))
            wait("vector", sL, ab_lds[-1])
            if from_dram:
                wait("sync", sT, mm_lm[KC - 1])
            lnb_lm = [0] * KC
            for k in range(KC):
                if from_dram:
                    if k >= 2:
                        wait("sync", sV, lnb_lm[k - 2])
                    ld = dma("sync", sL, xbuf[k % 2], src_tile_fn(k))
                    wait("vector", sL, ld)
                src_ap = xbuf[k % 2] if from_dram else src_tile_fn(k)
                lnb_lm[k] = op("vector", sV, "tensor_tensor", dst_off_fn(k),
                               src_ap, ab[0], op=OP.mult)
            for k in range(KC):
                op("vector", sV if k == KC - 1 else None, "tensor_tensor",
                   dst_off_fn(k), dst_off_fn(k), ab[1], op=OP.add)
            return sV.n

        def x_tile(k):
            return dram_ap(x_t, k * 128 * T, [[T, 128], [1, T]])

        wait("tensor", sS, tables_done)
        ln1_done = emit_ln(
            x_tile, lambda k: a1(k * 1024, 1024),
            [a2(0, 1024), a2(1024, 1024)],
            [a2(2048, 1024), a2(3072, 1024)],
            lambda off, n=512: a3row(off, n),
            [a2(2048, 1024), a2(3072, 1024)],
            True, 1024,
        )
        if stage < 5:
            wait("gpsimd", sV, ln1_done)
            for k in range(KC):
                dma("gpsimd", sG, dram_ap(dbg["xh"], k * 128 * T, [[T, 128], [1, T]]),
                    a1(k * 1024, 1024))
        if stage == 0:
            wait("gpsimd", sG)
            finish()
            return nc

        def wslot(k, n=512):
            return a2(k * 512, n)

        def wks(tp, i, n=512):
            return a2(8192 + tp * 2048 + i * 512, n)

        def emit_bias_bcast(bias_dram):
            ld = dma("sync", sL, a3(0, 2048),
                     bass.AP(bias_dram.tensor, 0, [[0, 128], [1, 2048]]))
            return ld

        def emit_rope_elu_tm(wa, wb, wc, wd, tau):
            sslice = stm[:, tau * 64:(tau + 1) * 64]
            cslice = ctm[:, tau * 64:(tau + 1) * 64]
            for hh in range(4):
                c0 = hh * 128
                k1 = ap_sub(wa, c0, 64)
                k2 = ap_sub(wa, c0 + 64, 64)
                op("vector", None, "tensor_tensor", ap_sub(wb, c0, 64), k1, cslice, op=OP.mult)
                op("vector", None, "tensor_tensor", ap_sub(wb, c0 + 64, 64), k1, sslice, op=OP.mult)
                op("vector", None, "tensor_tensor", ap_sub(wc, c0, 64), k2, sslice, op=OP.mult)
                op("vector", None, "tensor_tensor", ap_sub(wc, c0 + 64, 64), k2, cslice, op=OP.mult)
            f_n = 0
            for hh in range(4):
                c0 = hh * 128
                op("vector", None, "tensor_tensor", ap_sub(wd, c0, 64),
                   ap_sub(wb, c0, 64), ap_sub(wc, c0, 64), op=OP.subtract)
                f_n = op("vector", sV if hh == 3 else None, "tensor_tensor",
                         ap_sub(wd, c0 + 64, 64),
                         ap_sub(wb, c0 + 64, 64), ap_sub(wc, c0 + 64, 64), op=OP.add)
            wait("vector", sV, f_n)
            op("vector", None, "tensor_scalar", wb, wd, 0.0, None, op0=OP.max)
            v_n = op("vector", sV, "tensor_scalar", wc, wd, 0.0, None, op0=OP.min)
            wait("scalar", sV, v_n)
            s_n = op("scalar", sS, "activation", wc, wc, AF.Exp)
            wait("vector", sS, s_n)
            return op("vector", sV, "tensor_tensor", wa, wc, wb, op=OP.add)

        wks_spill = {0: 0, 1: 0}

        def emit_tm_pass(w_dram, bias_dram, kind, spill_dram):
            wait("sync", sT, sT.n)
            wait("sync", sV, sV.n)
            wait("sync", sG, sG.n)
            bb_done = emit_bias_bcast(bias_dram)
            last_mm = {}
            cons_lm = {}
            for oh in range(4):
                if oh > 0:
                    wait("sync", sT, last_mm[(oh - 1, NT - 1)])
                ld = 0
                for k in range(KC):
                    ld = dma("sync", sL, wslot(k),
                             dram_ap(w_dram, k * 128 * D + oh * 512, [[D, 128], [1, 512]]))
                wait("tensor", sL, ld)
                wait("tensor", sV, ln1_done)
                for t in range(NT):
                    p = pb[t % 2]
                    use_bank(t % 2)
                    for k in range(KC):
                        last_mm[(oh, t)] = op(
                            "tensor", sT if k == KC - 1 else None, "matmul",
                            p[0:128, 0:512], a1(k * 1024 + t * 128, 128), wslot(k),
                            start=(k == 0), stop=(k == KC - 1))
                    tp = t % 2
                    wa, wb, wc, wd = (wks(tp, 0), wks(tp, 1), wks(tp, 2), wks(tp, 3))
                    wait("vector", sT, last_mm[(oh, t)])
                    wait("vector", sL, bb_done)
                    wait("vector", sG, wks_spill[tp])
                    op("vector", None, "tensor_tensor", wa, p[0:128, 0:512],
                       a3(oh * 512, 512), op=OP.add)
                    if kind == "v":
                        fin = op("vector", sV, "nop")
                    else:
                        fin = emit_rope_elu_tm(wa, wb, wc, wd, t)
                    rd_bank(t % 2, sV, fin)
                    cons_lm[(oh, t)] = fin
                    if kind in ("k", "v"):
                        wait("gpsimd", sV, fin)
                        wks_spill[tp] = dma(
                            "gpsimd", sG,
                            dram_ap(spill_dram, t * 128 * D + oh * 512, [[D, 128], [1, 512]]),
                            wa)
                    else:
                        for hh in range(4):
                            bank = 4 + hh % 2
                            wait("tensor", sV, fin)
                            use_bank(bank)
                            t_n = op("tensor", sT, "transpose", pb[bank][0:128, 0:128],
                                     ap_sub(wa, hh * 128, 128), idn[:])
                            wait("vector", sT, t_n)
                            v_n = op("vector", sV, "tensor_copy",
                                     ap_sub(wb, hh * 128, 128), pb[bank][0:128, 0:128])
                            rd_bank(bank, sV, v_n)
                        cons_lm[(oh, t)] = sV.n
                        rd_bank(t % 2, sV, cons_lm[(oh, t)])
                        wait("gpsimd", sV, cons_lm[(oh, t)])
                        wks_spill[tp] = dma(
                            "gpsimd", sG,
                            dram_ap(spill_dram,
                                    (oh * 4) * NT * 128 * 128 + t * 128 * 128,
                                    [[128, 128], [NT * 128 * 128, 4], [1, 128]]),
                            bass.AP(wb.tensor, wb.offset, [list(wb.ap[0]), [128, 4], [1, 128]]))
            return sG.n

        emit_tm_pass(wk, biases["k"], "k", k_sp)
        emit_tm_pass(wv, biases["v"], "v", v_sp)
        kv_spill_done = sG.n

        if stage < 5:
            wait("gpsimd", sG, kv_spill_done)
            dma("gpsimd", sG, dram_ap(dbg["k"], 0, [[1, NT * 128 * D]]),
                dram_ap(k_sp, 0, [[1, NT * 128 * D]]))
            dma("gpsimd", sG, dram_ap(dbg["v"], 0, [[1, NT * 128 * D]]),
                dram_ap(v_sp, 0, [[1, NT * 128 * D]]))
        if stage == 1:
            dma("gpsimd", sG, dram_ap(dbg["tabs"], 0, [[1024, 128], [1, 512]]), stm[:])
            dma("gpsimd", sG, dram_ap(dbg["tabs"], 512, [[1024, 128], [1, 512]]), ctm[:])

        def kr(tp, off=0, n=2048, p=128, pbase=0):
            return bass.AP(A2[:].tensor, 8192 + tp * 2048 + off + pbase * A2W,
                           [[A2W, p], [1, n]])

        def vr(tp, off=0, n=2064, p=128, pbase=0):
            return bass.AP(A2[:].tensor, 12288 + tp * 2064 + off + pbase * A2W,
                           [[A2W, p], [1, n]])

        wait("vector", sG, kv_spill_done)
        for tp in range(2):
            for h in range(H):
                op("vector", None, "memset", vr(tp, h * 129 + 128, 1), 1.0)
        op("vector", None, "memset", S_all[:], 0.0)
        vring_ones = op("vector", sV, "nop")

        p3_mm = {}
        p3_add = {}
        for tau in range(NT):
            tp = tau % 2
            if tau >= 2:
                wait("sync", sT, p3_mm[(tau - 2, H - 1)])
            wait("sync", sV, vring_ones)
            dma("sync", sL, kr(tp), dram_ap(k_sp, tau * 128 * D, [[D, 128], [1, D]]))
            ld = dma("sync", sL,
                     bass.AP(A2[:].tensor, 12288 + tp * 2064, [[A2W, 128], [129, 16], [1, 128]]),
                     dram_ap(v_sp, tau * 128 * D, [[D, 128], [128, 16], [1, 128]]))
            wait("tensor", sL, ld)
            for h in range(H):
                bi = 4 + (tau * H + h) % 2
                use_bank(bi)
                p3_mm[(tau, h)] = op("tensor", sT, "matmul", pb[bi][0:128, 0:129],
                                     kr(tp, h * 128, 128), vr(tp, h * 129, 129),
                                     start=True, stop=True)
                wait("vector", sT, p3_mm[(tau, h)])
                p3_add[tau * H + h] = op(
                    "vector", sV, "tensor_tensor", S_all[:, h * 129:(h + 1) * 129],
                    S_all[:, h * 129:(h + 1) * 129], pb[bi][0:128, 0:129], op=OP.add)
                rd_bank(bi, sV, p3_add[tau * H + h])
        p3_done = sV.n

        wait("gpsimd", sV, p3_done)
        dma("gpsimd", sG, dram_ap(cc_in.ap(), 0, [[129, 128], [129 * 128, 16], [1, 129]]),
            bass.AP(S_all[:].tensor, 0, [[H * 129, 128], [129, 16], [1, 129]]))
        wait("gpsimd", sG)

        def f_cc(e):
            e.collective_compute(
                "AllGather", OP.bypass,
                replica_groups=[list(range(NCORES))],
                ins=[cc_in.ap().opt()],
                outs=[cc_out.ap().opt()],
            ).then_inc(sem_h[id(sC)], 1)
        emit("gpsimd", f_cc)
        sC.n += 1

        if stage in (2, 3, 4):
            dma("gpsimd", sG, dram_ap(dbg["S"], 0, [[129, 128], [129 * 128, 16], [1, 129]]),
                bass.AP(S_all[:].tensor, 0, [[H * 129, 128], [129, 16], [1, 129]]))

        def emit_fm_pass(w_dram, bname, spill_dram, func):
            bc = {"q": 0, "k": 16, "v": 32, "g": 48, "o": 64, "f": 80}[bname]
            mm_lm = {}
            sp_lm = {}
            act_lm = {}
            wait("sync", sT, sT.n)
            wait("sync", sV, sV.n)
            for oc in range(H):
                sl = oc % 2
                if oc >= 2:
                    wait("sync", sT, mm_lm[(oc - 2, 1)])
                ld = 0
                for k in range(KC):
                    ld = dma("sync", sL, a3(sl * 2048 + k * 128, 128),
                             dram_ap(w_dram, k * 128 * D + oc * 128, [[D, 128], [1, 128]]))
                wait("tensor", sL, ld)
                wait("tensor", sV, ln1_done)
                for th in range(2):
                    p = pb[2 + th]
                    use_bank(2 + th)
                    for k in range(KC):
                        mm_lm[(oc, th)] = op(
                            "tensor", sT if k == KC - 1 else None, "matmul",
                            p[0:128, 0:512], a3(sl * 2048 + k * 128, 128),
                            a1(k * 1024 + th * 512, 512),
                            start=(k == 0), stop=(k == KC - 1))
                    wait("scalar", sT, mm_lm[(oc, th)])
                    if oc >= 1:
                        wait("scalar", sG, sp_lm.get((oc - 1, th), 0))
                    act_lm[(oc, th)] = op("scalar", sS, "activation", a2(th * 512, 512),
                                          p[0:128, 0:512], func,
                                          bias=bcol[:, bc + oc:bc + oc + 1])
                    rd_bank(2 + th, sS, act_lm[(oc, th)])
                    wait("gpsimd", sS, act_lm[(oc, th)])
                    sp_lm[(oc, th)] = dma(
                        "gpsimd", sG,
                        dram_ap(spill_dram, oc * NT * 128 * 128 + th * 4 * 128 * 128,
                                [[128, 128], [128 * 128, 4], [1, 128]]),
                        bass.AP(A2[:].tensor, th * 512, [[A2W, 128], [128, 4], [1, 128]]))

        emit_fm_pass(wg, "g", g_sp, AF.Sigmoid)
        emit_tm_pass(wq, biases["q"], "q", q_sp)
        qg_spill_done = sG.n
        q_tensor_done = sT.n

        if stage < 5:
            wait("gpsimd", sG, qg_spill_done)
            dma("gpsimd", sG, dram_ap(dbg["q"], 0, [[1, H * NT * 128 * 128]]),
                dram_ap(q_sp, 0, [[1, H * NT * 128 * 128]]))
            dma("gpsimd", sG, dram_ap(dbg["g"], 0, [[1, H * NT * 128 * 128]]),
                dram_ap(g_sp, 0, [[1, H * NT * 128 * 128]]))
        if stage == 1:
            wait("gpsimd", sC, 1)
            wait("gpsimd", sG)
            finish()
            return nc

        op("vector", None, "memset", S_sb[:], 0.0)
        wait("gpsimd", sC, 1)
        wait("gpsimd", sT, q_tensor_done)

        def stslot(cp):
            return bass.AP(A1[:].tensor, (cp % 2) * 2064,
                           [[A1W, 128], [129, 16], [1, 129]])

        def gsrc(cp):
            return dram_ap(cc_out.ap(), cp * D * 129,
                           [[129, 128], [129 * 128, 16], [1, 129]])

        pm_ld = dma("gpsimd", sG, pmc8[:],
                    bass.AP(pmask_d.tensor, 0, [[0, 128], [1, NCORES]]))
        g_lms = [0] * NCORES
        for cp in (0, 1):
            g_lms[cp] = dma("gpsimd", sG, stslot(cp), gsrc(cp))
        wait("vector", sG, pm_ld)
        cons_lm = 0
        for cp in range(NCORES):
            wait("vector", sG, g_lms[cp])
            cons_lm = op("vector", sV, "scalar_tensor_tensor", S_sb[:],
                         stslot(cp), pmc8[:, cp:cp + 1], S_sb[:],
                         op0=OP.mult, op1=OP.add)
            if cp + 2 < NCORES:
                wait("gpsimd", sV, cons_lm)
                g_lms[cp + 2] = dma("gpsimd", sG, stslot(cp + 2), gsrc(cp + 2))
        seed_done = cons_lm

        if stage == 2:
            wait("gpsimd", sV, seed_done)
            dma("gpsimd", sG, dram_ap(dbg["Ssb"], 0, [[129, 128], [129 * 128, 16], [1, 129]]),
                bass.AP(S_sb[:].tensor, 0, [[H * 129, 128], [129, 16], [1, 129]]))
        if stage == 2:
            wait("gpsimd", sG)
            finish()
            return nc

        def fwh(h):
            return dram_ap(fw, (h // 2) * 128 * D + (h % 2) * 1024, [[D, 128], [1, 1024]])

        def a4(off, n):
            return bass.AP(A4[:].tensor, off, [[2048, 128], [1, n]])

        def sall4(off, n=1024):
            return bass.AP(S_all[:].tensor, off, [[H * 129, 128], [1, n]])

        sc_ld = {}
        sc_lm = {}
        t_ld = {}
        t_g = {}
        th_dv = [0]
        pa_v, pb_v = pad_scratch["vector"]
        pa_s, pb_s = pad_scratch["scalar"]
        if stage >= 5:
            op("vector", None, "memset", accc[:], 0.0)
            for h in (0, 1):
                sc_ld[h] = dma("gpsimd", sG, a4((h % 2) * 1024, 1024), fwh(h))

        def ffn_scale_half(h):
            wait("vector", sG, sc_ld[h])
            op("vector", None, "tensor_reduce", rcol[:], a4((h % 2) * 1024, 1024),
               axis=AX.X, op=OP.add, apply_absolute_value=True)
            pad("vector")
            pad("vector")
            pad("vector")
            op("vector", None, "tensor_copy", rcol2[:], rcol[:])
            pad("vector")
            pad("vector")
            lm = op("vector", sV, "tensor_tensor", accc[:], accc[:], rcol2[:], op=OP.add)
            sc_lm[h] = lm
            if h + 2 < 32:
                wait("gpsimd", sV, lm)
                sc_ld[h + 2] = dma("gpsimd", sG, a4((h % 2) * 1024, 1024), fwh(h + 2))

        def ffn_threshold():
            pad("vector")
            pad("vector")
            acc_done = op("vector", sV, "tensor_copy", pb_v, pa_v)
            wait("sync", sV, acc_done)
            l1 = dma("sync", sL, dram_ap(bc_d, 5120, [[1, 1], [1, 128]]), accc[:])
            wait("sync", sL, l1)
            l2 = dma("sync", sL, trow[:], dram_ap(bc_d, 5120, [[1, 1], [1, 128]]))
            wait("vector", sL, l2)
            op("vector", None, "tensor_reduce", tot[:], trow[:], axis=AX.X, op=OP.add)
            pad("vector")
            pad("vector")
            tot_n = op("vector", sV, "tensor_copy", pb_v, pa_v)
            wait("scalar", sV, tot_n)
            op("scalar", None, "activation", t11a[:], tot[:], AF.Identity,
               bias=c_e11[:], scale=1.0 / (1 << 23))
            op("scalar", None, "activation", t11s[:], tot[:], AF.Identity,
               scale=1.0 / (1 << 22))
            op("scalar", None, "activation", t11n[:], t11a[:], AF.Identity, scale=-1.0)
            pad("scalar")
            pad("scalar")
            s_n = op("scalar", sS, "activation", pb_s, pa_s, AF.Identity)
            th = 0
            for src, dst, boff in ((t11a, thc, 5504), (t11n, nthc, 5760),
                                   (t11s, scc, 5888)):
                th = emit_row_bcast(src[:], dst[:], 1, boff, sS, s_n)
            return th

        def ffn_tern_half(h):
            wait("vector", sG, t_ld[h])
            wait("vector", sL, th_dv[0])
            if h >= 1:
                wait("vector", sG, t_g[h - 1])
            op("vector", None, "tensor_scalar", sall4(0),
               a4((h % 2) * 1024, 1024), thc[:], None, op0=OP.is_gt)
            op("vector", None, "tensor_scalar", sall4(1024),
               a4((h % 2) * 1024, 1024), nthc[:], None, op0=OP.is_lt)
            pad("vector")
            op("vector", None, "tensor_tensor", sall4(0), sall4(0),
               sall4(1024), op=OP.subtract)
            pad("vector")
            pad("vector")
            lm = op("vector", sV, "tensor_copy", pb_v, pa_v)
            wait("gpsimd", sV, lm)
            t_g[h] = dma("gpsimd", sG,
                         dram_ap(wt_sp, (h // 2) * 128 * D + (h % 2) * 1024,
                                 [[D, 128], [1, 1024]]),
                         sall4(0))
            if h + 2 < 32:
                t_ld[h + 2] = dma("gpsimd", sG, a4((h % 2) * 1024, 1024), fwh(h + 2))

        def ffn_prep(b):
            if stage < 5:
                return
            if b < 8:
                for j in range(4):
                    ffn_scale_half(4 * b + j)
                return
            if b == 8:
                th_dv[0] = ffn_threshold()
                wait("gpsimd", sV, sc_lm[31])
                for h in (0, 1):
                    t_ld[h] = dma("gpsimd", sG, a4((h % 2) * 1024, 1024), fwh(h))
            for j in range(4):
                ffn_tern_half(4 * (b - 8) + j)

        def qr(tp, off=0, n=128, p=128, pbase=0):
            return bass.AP(A2[:].tensor, tp * 2048 + off + pbase * A2W,
                           [[A2W, p], [1, n]])

        def gr(tp, off=0, n=64):
            return bass.AP(A2[:].tensor, 4096 + tp * 2048 + off, [[A2W, 128], [1, n]])

        gate_lm = {}
        add_lm = {}
        att_lm = {}
        scan_ld = [0] * NT
        blk_gate = sV.n
        blk_tr = 0
        for tau in range(NT):
            tp = tau % 2
            if tau >= 2:
                wait("sync", sV, gate_lm[(tau - 2, 1)])
            else:
                wait("sync", sV, seed_done)
            dma("sync", sL, bass.AP(A2[:].tensor, tp * 2048, [[A2W, 128], [128, 16], [1, 128]]),
                dram_ap(q_sp, tau * 128 * 128, [[128, 128], [NT * 128 * 128, 16], [1, 128]]))
            dma("sync", sL, bass.AP(A2[:].tensor, 4096 + tp * 2048,
                                    [[A2W, 128], [128, 16], [1, 128]]),
                dram_ap(g_sp, tau * 128 * 128, [[128, 128], [NT * 128 * 128, 16], [1, 128]]))
            dma("sync", sL, kr(tp), dram_ap(k_sp, tau * 128 * D, [[D, 128], [1, D]]))
            scan_ld[tau] = dma(
                "sync", sL,
                bass.AP(A2[:].tensor, 12288 + tp * 2064, [[A2W, 128], [129, 16], [1, 128]]),
                dram_ap(v_sp, tau * 128 * D, [[D, 128], [128, 16], [1, 128]]))
            wait("tensor", sL, scan_ld[tau])
            for ic in range(2):
                add_half = blk_gate
                for half in range(2):
                    wait("tensor", sV, add_half)
                    ds_last = 0
                    for b in range(8):
                        h = half * 8 + b
                        ds_last = op("tensor", sT if b == 7 else None, "matmul",
                                     pb[b][0:128, 0:129],
                                     kr(tp, h * 128, 128, p=64, pbase=ic * 64),
                                     vr(tp, h * 129, 129, p=64, pbase=ic * 64),
                                     start=True, stop=True)
                    wait("vector", sT, ds_last)
                    for b in range(8):
                        h = half * 8 + b
                        add_half = op("vector", sV if b == 7 else None, "tensor_tensor",
                                      S_sb[:, h * 129:(h + 1) * 129],
                                      S_sb[:, h * 129:(h + 1) * 129],
                                      pb[b][0:128, 0:129], op=OP.add)
                pad("vector")
                pad("vector")
                pa, pb2 = pad_scratch["vector"]
                add_last = op("vector", sV, "tensor_copy", pb2, pa)
                ev_last = add_last
                for half in range(2):
                    wait("tensor", sV, ev_last)
                    nm_last = 0
                    for b in range(8):
                        h = half * 8 + b
                        nm_last = op("tensor", sT if b == 7 else None, "matmul",
                                     pb[b][0:64, 0:129],
                                     qr(tp, h * 128 + ic * 64, 64),
                                     S_sb[:, h * 129:(h + 1) * 129],
                                     start=True, stop=True)
                    wait("vector", sT, nm_last)
                    for b in range(8):
                        h = half * 8 + b
                        ev_last = op("vector", sV if b == 7 else None, "tensor_copy",
                                     bass.AP(A3[:].tensor, h * 129, [[A3W, 64], [1, 129]]),
                                     pb[b][0:64, 0:129])
                pad("vector")
                op("vector", None, "tensor_scalar", dnall[:],
                   bass.AP(A3[:].tensor, 128, [[A3W, 64], [129, 16]]),
                   1e-6, None, op0=OP.add)
                pad("vector")
                op("vector", None, "reciprocal", rcall[:], dnall[:])
                pad("vector")
                att_last = 0
                for h in range(H):
                    att_last = op("vector", sV if h == H - 1 else None, "tensor_tensor",
                                  bass.AP(A3[:].tensor, 2064 + h * 128, [[A3W, 64], [1, 128]]),
                                  bass.AP(A3[:].tensor, h * 129, [[A3W, 64], [1, 128]]),
                                  bass.AP(rcall[:].tensor, h, [[16, 64], [0, 128]]),
                                  op=OP.mult)
                att_lm[(tau, ic)] = att_last
                if stage in (3, 4) and tau == 0 and ic == 0:
                    wait("vector", sG, s_cp)
                    op("vector", None, "tensor_copy", dbg_h0_sb[:], S_sb[0:128, 0:256])
                    op("vector", None, "tensor_copy",
                       bass.AP(dbg_num_sb[:].tensor, 0, [[516, 64], [1, 129]]),
                       bass.AP(A3[:].tensor, 0, [[A3W, 64], [1, 129]]))
                    op("vector", None, "tensor_copy",
                       bass.AP(dbg_num_sb[:].tensor, 129, [[516, 64], [1, 16]]),
                       dnall[:])
                    op("vector", None, "tensor_copy",
                       bass.AP(dbg_num_sb[:].tensor, 145, [[516, 64], [1, 16]]),
                       rcall[:])
                    op("vector", None, "tensor_copy",
                       bass.AP(dbg_num_sb[:].tensor, 161, [[516, 64], [1, 128]]),
                       bass.AP(A3[:].tensor, 2064, [[A3W, 64], [1, 128]]))
                g_last = 0
                for half in range(2):
                    wait("tensor", sV, att_last if half == 0 else g_last)
                    tr_last = 0
                    for b in range(8):
                        h = half * 8 + b
                        tr_last = op("tensor", sT if b == 7 else None, "transpose",
                                     pb[b][0:128, 0:64],
                                     bass.AP(A3[:].tensor, 2064 + h * 128,
                                             [[A3W, 64], [1, 128]]),
                                     idn[0:64, 0:64])
                    wait("vector", sT, tr_last)
                    for b in range(8):
                        h = half * 8 + b
                        g_last = op("vector", sV if b == 7 else None, "tensor_tensor",
                                    a1(h * 1024 + tau * 128 + ic * 64, 64),
                                    pb[b][0:128, 0:64],
                                    gr(tp, h * 128 + ic * 64, 64),
                                    op=OP.mult)
                blk_gate = g_last
                gate_lm[(tau, ic)] = g_last
                ffn_prep(tau * 2 + ic)
        scan_done = sV.n

        if stage in (3, 4):
            wait("gpsimd", sV, scan_done)
            dma("gpsimd", sG, dram_ap(dbg["num"], 0, [[129 * 4, 64], [1, 129 * 4]]),
                bass.AP(dbg_num_sb[:].tensor, 0, [[129 * 4, 64], [1, 129 * 4]]))
            dma("gpsimd", sG, dram_ap(dbg["h0"], 0, [[256, 128], [1, 256]]),
                dbg_h0_sb[:])
            dma("gpsimd", sG, dram_ap(dbg["Ssb"], 0, [[129, 128], [129 * 128, 16], [1, 129]]),
                bass.AP(S_sb[:].tensor, 0, [[H * 129, 128], [129, 16], [1, 129]]))
            for k in range(KC):
                dma("gpsimd", sG, dram_ap(dbg["attn"], k * 128 * T, [[T, 128], [1, T]]),
                    a1(k * 1024, 1024))
        if stage == 3:
            wait("gpsimd", sG)
            finish()
            return nc

        wo_mm = {}
        wo_fin = {}
        wait("sync", sV, scan_done)
        for oc in range(H):
            sl = oc % 2
            if oc >= 2:
                wait("sync", sT, wo_mm[(oc - 2, 1)])
            if oc >= 1:
                wait("sync", sV, wo_fin.get((oc - 1) * 2 + 1, 0))
            for k in range(KC):
                dma("sync", sL, a3(sl * 2048 + k * 128, 128),
                    dram_ap(wo, k * 128 * D + oc * 128, [[D, 128], [1, 128]]))
            for th in range(2):
                dma("sync", sL, a3(4096 + th * 512, 512),
                    dram_ap(x_t, oc * 128 * T + th * 512, [[T, 128], [1, 512]]))
            ld = sL.n
            wait("tensor", sL, ld)
            wait("tensor", sV, scan_done)
            for th in range(2):
                bi = (oc * 2 + th) % 4
                use_bank(bi)
                for k in range(KC):
                    wo_mm[(oc, th)] = op(
                        "tensor", sT if k == KC - 1 else None, "matmul",
                        pb[bi][0:128, 0:512], a3(sl * 2048 + k * 128, 128),
                        a1(k * 1024 + th * 512, 512),
                        start=(k == 0), stop=(k == KC - 1))
                wait("scalar", sT, wo_mm[(oc, th)])
                if oc >= 1:
                    wait("scalar", sV, wo_fin.get((oc - 1) * 2 + th, 0))
                s_n = op("scalar", sS, "activation", a3(5120 + th * 512, 512),
                         pb[bi][0:128, 0:512], AF.Identity, bias=bcol[:, 64 + oc:64 + oc + 1])
                rd_bank(bi, sS, s_n)
                wait("vector", sS, s_n)
                wait("vector", sL, ld)
                wo_fin[oc * 2 + th] = op(
                    "vector", sV, "tensor_tensor", a2(oc * 1024 + th * 512, 512),
                    a3(5120 + th * 512, 512), a3(4096 + th * 512, 512), op=OP.add)
        x2_done = sV.n

        if stage == 4:
            wait("gpsimd", sV, x2_done)
            for k in range(KC):
                dma("gpsimd", sG, dram_ap(dbg["x2"], k * 128 * T, [[T, 128], [1, T]]),
                    a2(k * 1024, 1024))
            wait("gpsimd", sG)
            finish()
            return nc

        wait("tensor", sV, x2_done)
        wait("scalar", sV, x2_done)
        ln2_done = emit_ln(
            lambda k: a2(k * 1024, 1024), lambda k: a1(k * 1024, 1024),
            None,
            [a1(0, 1024), a1(1024, 1024)],
            lambda off, n=512: a3row(2048 + off, n),
            [a3(0, 1024), a3(1024, 1024)],
            False, 3072,
        )

        slot_mm = {0: 0, 1: 0}
        ff_fin = {}
        st_lm = {0: 0, 1: 0}
        wait("sync", sG, t_g[31])
        wait("sync", sV, ln2_done)
        wait("tensor", sV, ln2_done)
        for pi in range(4):
            th, og = pi // 2, pi % 2
            for k in range(KC):
                slw = k % 2
                wait("sync", sT, slot_mm[slw])
                ld = dma("sync", sL, a3(slw * 1024, 1024),
                         dram_ap(wt_sp, k * 128 * D + og * 1024, [[D, 128], [1, 1024]]))
                wait("tensor", sL, ld)
                for oc in range(8):
                    if k == 0:
                        use_bank(oc)
                    slot_mm[slw] = op(
                        "tensor", sT if oc == 7 else None, "matmul",
                        pb[oc][0:128, 0:512], a3(slw * 1024 + oc * 128, 128),
                        a1(k * 1024 + th * 512, 512),
                        start=(k == 0), stop=(k == KC - 1))
            for oc in range(8):
                o_g = og * 8 + oc
                ft = bass.AP(ftmp[:].tensor, (oc % 2) * 512, [[1024, 128], [1, 512]])
                wait("scalar", sT, slot_mm[(KC - 1) % 2])
                wait("scalar", sL, th_dv[0])
                if pi > 0 or oc >= 2:
                    wait("scalar", sG, st_lm[oc % 2])
                s_n = op("scalar", sS, "activation", ft, pb[oc][0:128, 0:512],
                         AF.Identity, bias=bcol[:, 80 + o_g:80 + o_g + 1],
                         scale=scc[:])
                rd_bank(oc, sS, s_n)
                wait("vector", sS, s_n)
                op("vector", None, "tensor_tensor", ft, ft,
                   a2(o_g * 1024 + th * 512, 512), op=OP.add)
                pad("vector")
                pad("vector")
                ff_fin[(pi, oc)] = op("vector", sV, "tensor_copy", pb_v, pa_v)
                wait("gpsimd", sV, ff_fin[(pi, oc)])
                st_lm[oc % 2] = dma("gpsimd", sG,
                                    dram_ap(y_t, o_g * 128 * T + th * 512, [[T, 128], [1, 512]]),
                                    ft)

        wait("gpsimd", sG)
        finish()
    return nc


_ROPE_PERM = None


def _rope_perm():
    global _ROPE_PERM
    if _ROPE_PERM is None:
        p = []
        for h in range(H):
            base = h * DK
            p.extend(base + np.arange(0, DK, 2))
            p.extend(base + np.arange(1, DK, 2))
        _ROPE_PERM = np.asarray(p)
    return _ROPE_PERM


def _host_prep(inputs):
    perm = _rope_perm()
    f32 = np.float32
    x = np.asarray(inputs["L0_attn_x"], f32)
    wqT = np.ascontiguousarray(np.asarray(inputs["L0_attn_Wq"], f32).T[:, perm])
    wkT = np.ascontiguousarray(np.asarray(inputs["L0_attn_Wk"], f32).T[:, perm])
    wvT = np.ascontiguousarray(np.asarray(inputs["L0_attn_Wv"], f32).T)
    wgT = np.ascontiguousarray(np.asarray(inputs["L0_attn_Wg"], f32).T)
    woT = np.ascontiguousarray(np.asarray(inputs["L0_attn_Wo"], f32).T)
    fwT = np.ascontiguousarray(np.asarray(inputs["L0_ffn_weight"], f32).T)
    bq = np.ascontiguousarray(np.asarray(inputs["L0_attn_bq"], f32)[perm])
    bk = np.ascontiguousarray(np.asarray(inputs["L0_attn_bk"], f32)[perm])
    ident = np.eye(128, dtype=f32)
    xT = np.ascontiguousarray(x[0].T)
    in_maps = []
    for c in range(NCORES):
        in_maps.append(dict(
            x_t=np.ascontiguousarray(xT[:, c * T:(c + 1) * T]),
            wq=wqT, wk=wkT, wv=wvT, wg=wgT, wo=woT, fw=fwT,
            bq=bq, bk=bk,
            bv=np.asarray(inputs["L0_attn_bv"], f32),
            bg=np.asarray(inputs["L0_attn_bg"], f32),
            bo=np.asarray(inputs["L0_attn_bo"], f32),
            bf=np.asarray(inputs["L0_ffn_bias"], f32),
            pos=np.arange(c * T, (c + 1) * T, dtype=f32).reshape(1, T),
            jrow=np.arange(64, dtype=f32).reshape(1, 64),
            ident=ident,
            pmask=(np.arange(NCORES) < c).astype(f32).reshape(1, NCORES),
        ))
    return in_maps


def kernel(**inputs):
    in_maps = _host_prep(inputs)
    nc = _build(stage=5)
    res = run_bass_kernel_spmd(nc, in_maps, core_ids=list(range(NCORES)))
    out = np.empty((1, NCORES * T, D), np.float32)
    for c in range(NCORES):
        out[0, c * T:(c + 1) * T, :] = res.results[c]["y_t"].T
    return out


# version 14
# speedup vs baseline: 1.1695x; 1.0234x over previous
import numpy as np
from contextlib import ExitStack

import concourse.bass as bass
import concourse.mybir as mybir
from concourse.bass_utils import run_bass_kernel_spmd

F32 = mybir.dt.float32
I32 = mybir.dt.int32
AF = mybir.ActivationFunctionType
OP = mybir.AluOpType
AX = mybir.AxisListType

NCORES = 8
D = 2048
T = 1024
H = 16
DK = 128
CS = 64
NCH = T // CS
NT = T // 128
KC = D // 128
A1W, A2W, A3W = 16384, 16512, 6144


def _split3(v):
    a = np.float32(v)
    b = np.float32(v - float(a))
    c = np.float32(v - float(a) - float(b))
    return float(a), float(b), float(c)


TWO_PI = 2 * np.pi
C1 = 6.28125
C2 = float(np.float32(TWO_PI - C1))
C3 = float(np.float32(TWO_PI - C1 - C2))
INV_2PI = float(np.float32(1.0 / TWO_PI))
L1, L2, L3 = _split3(np.log(10000.0) / 64.0)
HPI1, HPI2, _ = _split3(np.pi / 2)


class _Ctr:
    def __init__(self):
        self.n = 0


def _build(stage=5):
    nc = bass.Bass("TRN2", target_bir_lowering=False, detect_race_conditions=False)

    x_t = nc.dram_tensor("x_t", [D, T], F32, kind="ExternalInput").ap()
    wq = nc.dram_tensor("wq", [D, D], F32, kind="ExternalInput").ap()
    wk = nc.dram_tensor("wk", [D, D], F32, kind="ExternalInput").ap()
    wv = nc.dram_tensor("wv", [D, D], F32, kind="ExternalInput").ap()
    wg = nc.dram_tensor("wg", [D, D], F32, kind="ExternalInput").ap()
    wo = nc.dram_tensor("wo", [D, D], F32, kind="ExternalInput").ap()
    fw = nc.dram_tensor("fw", [D, D], F32, kind="ExternalInput").ap()
    biases = {
        nm: nc.dram_tensor("b" + nm, [D], F32, kind="ExternalInput").ap()
        for nm in ("q", "k", "v", "g", "o", "f")
    }
    pos_d = nc.dram_tensor("pos", [1, T], F32, kind="ExternalInput").ap()
    jrow_d = nc.dram_tensor("jrow", [1, 64], F32, kind="ExternalInput").ap()
    ident_d = nc.dram_tensor("ident", [128, 128], F32, kind="ExternalInput").ap()
    pmask_d = nc.dram_tensor("pmask", [1, NCORES], F32, kind="ExternalInput").ap()

    y_t = nc.dram_tensor("y_t", [D, T], F32, kind="ExternalOutput").ap()

    q_sp = nc.dram_tensor("q_sp", [H * NT * 128, 128], F32).ap()
    g_sp = nc.dram_tensor("g_sp", [H * NT * 128, 128], F32).ap()
    k_sp = nc.dram_tensor("k_sp", [NT * 128, D], F32).ap()
    v_sp = nc.dram_tensor("v_sp", [NT * 128, D], F32).ap()
    wt_sp = nc.dram_tensor("wt_sp", [D, D], F32).ap()
    bc_d = nc.dram_tensor("bc_d", [1, 8192], F32).ap()
    cc_in = nc.dram_tensor("cc_in", [D, 129], F32)
    cc_out = nc.dram_tensor("cc_out", [NCORES * D, 129], F32)

    dbg = {}
    if stage < 5:
        dbg["tabs"] = nc.dram_tensor("dbg_tabs", [128, 1024], F32, kind="ExternalOutput").ap()
        dbg["xh"] = nc.dram_tensor("dbg_xh", [D, T], F32, kind="ExternalOutput").ap()
        dbg["k"] = nc.dram_tensor("dbg_k", [NT * 128, D], F32, kind="ExternalOutput").ap()
        dbg["v"] = nc.dram_tensor("dbg_v", [NT * 128, D], F32, kind="ExternalOutput").ap()
        dbg["q"] = nc.dram_tensor("dbg_q", [H * NT * 128, 128], F32, kind="ExternalOutput").ap()
        dbg["g"] = nc.dram_tensor("dbg_g", [H * NT * 128, 128], F32, kind="ExternalOutput").ap()
    if stage in (2, 3, 4):
        dbg["S"] = nc.dram_tensor("dbg_S", [D, 129], F32, kind="ExternalOutput").ap()
        dbg["Ssb"] = nc.dram_tensor("dbg_Ssb", [D, 129], F32, kind="ExternalOutput").ap()
    if stage in (3, 4):
        dbg["attn"] = nc.dram_tensor("dbg_attn", [D, T], F32, kind="ExternalOutput").ap()
        dbg["num"] = nc.dram_tensor("dbg_num", [64, 129 * 4], F32, kind="ExternalOutput").ap()
        dbg["h0"] = nc.dram_tensor("dbg_h0", [128, 256], F32, kind="ExternalOutput").ap()
    if stage == 4:
        dbg["x2"] = nc.dram_tensor("dbg_x2", [D, T], F32, kind="ExternalOutput").ap()
    if stage == 6:
        dbg["ffn"] = nc.dram_tensor("dbg_ffn", [128, 8], F32, kind="ExternalOutput").ap()

    prog = {e: [] for e in ("sync", "tensor", "vector", "scalar", "gpsimd")}
    sL, sG, sT, sV, sS, sC = (_Ctr() for _ in range(6))
    sem_h = {}
    waited = {}

    def emit(eng, fn):
        prog[eng].append(fn)

    def wait(eng, ctr, val=None):
        v = ctr.n if val is None else val
        if v <= 0:
            return
        key = (eng, id(ctr))
        if waited.get(key, -1) >= v:
            return
        waited[key] = v
        emit(eng, lambda e, v=v, c=ctr: e.wait_ge(sem_h[id(c)], v))

    def dma(eng, ctr, out, in_, slow=False):
        ctr.n += 16
        emit(eng, lambda e, out=out, in_=in_, c=ctr, s=slow:
             e.dma_start(out=out, in_=in_, allow_slow_non_contiguous=s)
             .then_inc(sem_h[id(c)], 16))
        return ctr.n

    fence_scratch = {}

    def op(eng, ctr, meth, *a, **kw):
        fence = kw.pop("fence", True)
        if ctr is None:
            emit(eng, lambda e, meth=meth, a=a, kw=kw: getattr(e, meth)(*a, **kw))
            return None
        ctr.n += 1
        if not fence or eng in ("sync", "tensor"):
            def f(e, meth=meth, a=a, kw=kw, c=ctr):
                getattr(e, meth)(*a, **kw).then_inc(sem_h[id(c)], 1)
            emit(eng, f)
            return ctr.n
        emit(eng, lambda e, meth=meth, a=a, kw=kw: getattr(e, meth)(*a, **kw))
        fa, fb = fence_scratch[eng]

        def g(e, c=ctr, eng=eng, fa=fa, fb=fb):
            if eng == "scalar":
                ins = e.activation(fb, fa, AF.Identity)
            else:
                ins = e.tensor_copy(fb, fa)
            ins.then_inc(sem_h[id(c)], 1)
        emit(eng, g)
        return ctr.n

    bank_rd = {}

    def use_bank(b):
        g = bank_rd.get(b)
        if g:
            wait("tensor", g[0], g[1])

    def rd_bank(b, ctr, val):
        bank_rd[b] = (ctr, val)

    def pad(eng):
        fa, fb = pad_scratch[eng]
        if eng == "scalar":
            emit(eng, lambda e, fa=fa, fb=fb: e.activation(fb, fa, AF.Identity))
        else:
            emit(eng, lambda e, fa=fa, fb=fb: e.tensor_copy(fb, fa))

    with ExitStack() as ctx:
        def sbuf(name, shape, dt=F32):
            return ctx.enter_context(nc.sbuf_tensor(name, shape, dt))

        A1 = sbuf("A1", [128, A1W])
        A2 = sbuf("A2", [128, A2W])
        A3 = sbuf("A3", [128, A3W])
        kint = sbuf("kint", [128, 64], I32)
        stm = sbuf("stm", [128, 512])
        ctm = sbuf("ctm", [128, 512])
        S_all = sbuf("S_all", [128, H * 129])
        S_sb = sbuf("S_sb", [128, H * 129])
        A4 = sbuf("A4", [128, 2048])
        t11s = sbuf("t11s", [1, 1])
        nscc = sbuf("nscc", [128, 1])
        bcol = sbuf("bcol", [128, 96])
        irow = sbuf("irow", [1, 64])
        s_e = sbuf("s_e", [1, 64])
        ones_r = sbuf("ones_r", [1, 128])
        ones_c = sbuf("ones_c", [128, 1])
        idn = sbuf("idn", [128, 128])
        posc = sbuf("posc", [128, NT])
        ivfb = sbuf("ivfb", [128, 64])
        atm = sbuf("atm", [128, 64])
        atmc = sbuf("atmc", [128, 64])
        mtm = sbuf("mtm", [128, 64])
        mtk = sbuf("mtk", [128, 64])
        mta = sbuf("mta", [128, 64])
        accc = sbuf("accc", [128, 1])
        rcol = sbuf("rcol", [128, 1])
        rcol2 = sbuf("rcol2", [128, 1])
        thc = sbuf("thc", [128, 1])
        nthc = sbuf("nthc", [128, 1])
        scc = sbuf("scc", [128, 1])
        t11a = sbuf("t11a", [1, 1])
        t11b = sbuf("t11b", [1, 1])
        t11n = sbuf("t11n", [1, 1])
        tot = sbuf("tot", [1, 1])
        trow = sbuf("trow", [1, 128])
        pmc = sbuf("pmc", [128, 1])
        pmc8 = sbuf("pmc8", [128, NCORES])
        c_eps = sbuf("c_eps", [1, 1])
        c_e11 = sbuf("c_e11", [1, 1])
        dnall = sbuf("dnall", [64, 16])
        rcall = sbuf("rcall", [64, 16])
        dnc = sbuf("dnc", [64, 1])
        rcc = sbuf("rcc", [64, 1])
        ftmp = sbuf("ftmp", [128, 1024])
        if stage in (3, 4):
            dbg_num_sb = sbuf("dbg_num_sb", [64, 129 * 4])
            dbg_h0_sb = sbuf("dbg_h0_sb", [128, 256])
        if stage in (3, 4):
            pass
        pad_scratch = {}
        for _e in ("vector", "scalar", "gpsimd"):
            _fa = sbuf("fence_a_" + _e, [1, 1])
            _fb = sbuf("fence_b_" + _e, [1, 1])
            fence_scratch[_e] = (_fa[:], _fb[:])
            _pa = sbuf("pad_a_" + _e, [64, 64])
            _pb2 = sbuf("pad_b_" + _e, [64, 64])
            pad_scratch[_e] = (_pa[:], _pb2[:])

        pb = [ctx.enter_context(nc.psum_tensor("pb%d" % i, [128, 512], F32))
              for i in range(8)]

        for name, c in (("semL", sL), ("semG", sG), ("semT", sT),
                        ("semV", sV), ("semS", sS), ("semC", sC)):
            sem_h[id(c)] = ctx.enter_context(nc.semaphore(name))

        def a1(off, n):
            return bass.AP(A1[:].tensor, off, [[A1W, 128], [1, n]])

        def a2(off, n, p=128, pbase=0):
            return bass.AP(A2[:].tensor, off + pbase * A2W, [[A2W, p], [1, n]])

        def a3(off, n):
            return bass.AP(A3[:].tensor, off, [[A3W, 128], [1, n]])

        def a3row(off, n=512):
            return bass.AP(A3[:].tensor, off, [[A3W, 1], [1, n]])

        def ap_sub(ap, doff, n):
            return bass.AP(ap.tensor, ap.offset + doff, [list(ap.ap[0]), [1, n]])

        def dram_ap(d, offset, dims):
            return bass.AP(d.tensor, offset, dims)

        def finish():
            with nc.Block() as blk:
                for name, fns in prog.items():
                    def run_all(e, fns=fns):
                        for f in fns:
                            f(e)
                    getattr(blk, name)(run_all)

        dma("sync", sL, idn[:], ident_d)
        for i, nm in enumerate(("q", "k", "v", "g", "o", "f")):
            dma("sync", sL, bass.AP(bcol[:].tensor, i * 16, [[96, 128], [1, 16]]),
                dram_ap(biases[nm], 0, [[1, 128], [128, 16]]), slow=True)
        dma("sync", sL, posc[:], dram_ap(pos_d, 0, [[1, 128], [128, NT]]), slow=True)
        p0_loads = sL.n

        op("vector", None, "memset", ones_r[:], 1.0)
        op("vector", None, "memset", ones_c[:], 1.0)
        op("vector", None, "memset", c_eps[:], 1e-5)
        op("vector", None, "memset", c_e11[:], 5e-11)
        for _e in ("vector", "scalar", "gpsimd"):
            op("vector", None, "memset", fence_scratch[_e][0], 0.0)
            op("vector", None, "memset", pad_scratch[_e][0], 0.0)
        if stage in (3, 4):
            op("vector", None, "memset", dbg_h0_sb[:], 0.0)
            op("vector", None, "memset", dbg_num_sb[:], 0.0)
        jld = dma("sync", sL, s_e[:], jrow_d)
        wait("vector", sL, jld)
        op("vector", None, "tensor_scalar", irow[:], s_e[:], -L1, None, op0=OP.mult)
        op("vector", None, "scalar_tensor_tensor", irow[:], s_e[:], -L2, irow[:],
           op0=OP.mult, op1=OP.add)
        v_n = op("vector", sV, "scalar_tensor_tensor", irow[:], s_e[:], -L3, irow[:],
                 op0=OP.mult, op1=OP.add)
        wait("scalar", sV, v_n)
        s_n = op("scalar", sS, "activation", irow[:], irow[:], AF.Exp)
        ivf_done = s_n

        def emit_row_bcast(row_ap, out_ap, n, boff, prod_ctr, prod_val):
            wait("sync", prod_ctr, prod_val)
            l1 = dma("sync", sL, dram_ap(bc_d, boff, [[1, 1], [1, n]]), row_ap)
            wait("sync", sL, l1)
            return dma("sync", sL, out_ap,
                       bass.AP(bc_d.tensor, boff, [[0, out_ap.ap[0][1]], [1, n]]))

        ivfb_ld = emit_row_bcast(irow[:], ivfb[:], 64, 0, sS, ivf_done)
        wait("vector", sL, ivfb_ld)

        wait("vector", sL, p0_loads)
        for tau in range(NT):
            wait("vector", sS, sS.n)
            op("vector", None, "tensor_scalar", atm[:], ivfb[:], posc[:, tau:tau + 1],
               None, op0=OP.mult)
            op("vector", None, "tensor_scalar", atmc[:], atm[:], HPI1, HPI2,
               op0=OP.add, op1=OP.add)
            for src, dstt in ((atm, stm), (atmc, ctm)):
                op("vector", None, "tensor_scalar", mtm[:], src[:], INV_2PI, None, op0=OP.mult)
                op("vector", None, "tensor_copy", kint[:], mtm[:])
                op("vector", None, "tensor_copy", mtk[:], kint[:])
                op("vector", None, "tensor_tensor", mtm[:], mtm[:], mtk[:], op=OP.subtract)
                op("vector", None, "tensor_scalar", mta[:], mtm[:], 0.499999, None, op0=OP.is_gt)
                op("vector", None, "tensor_tensor", mtk[:], mtk[:], mta[:], op=OP.add)
                op("vector", None, "tensor_scalar", mta[:], mtm[:], -0.499999, None, op0=OP.is_lt)
                op("vector", None, "tensor_tensor", mtk[:], mtk[:], mta[:], op=OP.subtract)
                op("vector", None, "scalar_tensor_tensor", src[:], mtk[:], -C1, src[:],
                   op0=OP.mult, op1=OP.add)
                op("vector", None, "scalar_tensor_tensor", src[:], mtk[:], -C2, src[:],
                   op0=OP.mult, op1=OP.add)
                op("vector", None, "scalar_tensor_tensor", src[:], mtk[:], -C3, src[:],
                   op0=OP.mult, op1=OP.add)
                v_n = op("vector", sV, "tensor_scalar", src[:], src[:], 3.141592,
                         -3.141592, op0=OP.min, op1=OP.max)
                wait("scalar", sV, v_n)
                op("scalar", sS, "activation", dstt[:, tau * 64:(tau + 1) * 64], src[:], AF.Sin)
        tables_done = sS.n
        if stage in (-1, 0):
            wait("gpsimd", sS, tables_done)
            dma("gpsimd", sG, dram_ap(dbg["tabs"], 0, [[1024, 128], [1, 512]]), stm[:])
            dma("gpsimd", sG, dram_ap(dbg["tabs"], 512, [[1024, 128], [1, 512]]), ctm[:])
        if stage == -1:
            wait("gpsimd", sG)
            finish()
            return nc

        def emit_ln(src_tile_fn, dst_off_fn, xbuf, sq, srow, ab, from_dram, bc_base):
            mm_lm = [0] * KC
            sq_lm = [0] * KC
            for k in range(KC):
                if from_dram:
                    if k >= 2:
                        wait("sync", sT, mm_lm[k - 2])
                    ld = dma("sync", sL, xbuf[k % 2], src_tile_fn(k))
                src_ap = xbuf[k % 2] if from_dram else src_tile_fn(k)
                if k >= 2:
                    wait("scalar", sT, mm_lm[k - 2])
                if from_dram:
                    wait("scalar", sL, ld)
                sq_lm[k] = op("scalar", sS, "activation", sq[k % 2], src_ap, AF.Square)
                wait("tensor", sS, sq_lm[k])
                if k == 0:
                    for b in range(4):
                        use_bank(b)
                for th in range(2):
                    op("tensor", None, "matmul", pb[th][0:1, 0:512], ones_c[:],
                       ap_sub(src_ap, th * 512, 512), start=(k == 0), stop=(k == KC - 1))
                op("tensor", None, "matmul", pb[2][0:1, 0:512], ones_c[:],
                   ap_sub(sq[k % 2], 0, 512), start=(k == 0), stop=(k == KC - 1))
                mm_lm[k] = op("tensor", sT, "matmul", pb[3][0:1, 0:512], ones_c[:],
                              ap_sub(sq[k % 2], 512, 512), start=(k == 0), stop=(k == KC - 1))
            wait("vector", sT, mm_lm[KC - 1])
            for th in range(2):
                op("vector", None, "tensor_scalar", srow(th * 512, 512), pb[th][0:1, 0:512],
                   1.0 / D, None, op0=OP.mult)
                op("vector", None, "tensor_tensor", srow(1024 + th * 512, 512),
                   srow(th * 512, 512), srow(th * 512, 512), op=OP.mult)
                v_n = op("vector", sV, "scalar_tensor_tensor", srow(2048 + th * 512, 512),
                         pb[2 + th][0:1, 0:512], 1.0 / D, srow(1024 + th * 512, 512),
                         op0=OP.mult, op1=OP.subtract)
                rd_bank(th, sV, v_n)
                rd_bank(2 + th, sV, v_n)
                wait("scalar", sV, v_n)
                s_n = op("scalar", sS, "activation", srow(1024 + th * 512, 512),
                         srow(2048 + th * 512, 512), AF.Sqrt, bias=c_eps[:])
                wait("vector", sS, s_n)
                op("vector", None, "reciprocal", srow(2048 + th * 512, 512),
                   srow(1024 + th * 512, 512))
                op("vector", sV, "scalar_tensor_tensor", srow(3072 + th * 512, 512),
                   srow(th * 512, 512), -1.0, srow(2048 + th * 512, 512),
                   op0=OP.mult, op1=OP.mult)
            ab_lds = []
            for i, soff in enumerate((2048, 3072)):
                for th in range(2):
                    ab_lds.append(emit_row_bcast(
                        srow(soff + th * 512, 512), ap_sub(ab[i], th * 512, 512),
                        512, bc_base + (i * 2 + th) * 512, sV, sV.n))
            wait("vector", sL, ab_lds[-1])
            if from_dram:
                wait("sync", sT, mm_lm[KC - 1])
            lnb_lm = [0] * KC
            for k in range(KC):
                if from_dram:
                    if k >= 2:
                        wait("sync", sV, lnb_lm[k - 2])
                    ld = dma("sync", sL, xbuf[k % 2], src_tile_fn(k))
                    wait("vector", sL, ld)
                src_ap = xbuf[k % 2] if from_dram else src_tile_fn(k)
                lnb_lm[k] = op("vector", sV, "tensor_tensor", dst_off_fn(k),
                               src_ap, ab[0], op=OP.mult)
            for k in range(KC):
                op("vector", sV if k == KC - 1 else None, "tensor_tensor",
                   dst_off_fn(k), dst_off_fn(k), ab[1], op=OP.add)
            return sV.n

        def x_tile(k):
            return dram_ap(x_t, k * 128 * T, [[T, 128], [1, T]])

        wait("tensor", sS, tables_done)
        ln1_done = emit_ln(
            x_tile, lambda k: a1(k * 1024, 1024),
            [a2(0, 1024), a2(1024, 1024)],
            [a2(2048, 1024), a2(3072, 1024)],
            lambda off, n=512: a3row(off, n),
            [a2(2048, 1024), a2(3072, 1024)],
            True, 1024,
        )
        if stage < 5:
            wait("gpsimd", sV, ln1_done)
            for k in range(KC):
                dma("gpsimd", sG, dram_ap(dbg["xh"], k * 128 * T, [[T, 128], [1, T]]),
                    a1(k * 1024, 1024))
        if stage == 0:
            wait("gpsimd", sG)
            finish()
            return nc

        def wslot(k, n=512):
            return a2(k * 512, n)

        def wks(tp, i, n=512):
            return a2(8192 + tp * 2048 + i * 512, n)

        def emit_bias_bcast(bias_dram):
            ld = dma("sync", sL, a3(0, 2048),
                     bass.AP(bias_dram.tensor, 0, [[0, 128], [1, 2048]]))
            return ld

        def emit_rope_elu_tm(wa, wb, wc, wd, tau):
            sslice = stm[:, tau * 64:(tau + 1) * 64]
            cslice = ctm[:, tau * 64:(tau + 1) * 64]
            for hh in range(4):
                c0 = hh * 128
                k1 = ap_sub(wa, c0, 64)
                k2 = ap_sub(wa, c0 + 64, 64)
                op("vector", None, "tensor_tensor", ap_sub(wb, c0, 64), k1, cslice, op=OP.mult)
                op("vector", None, "tensor_tensor", ap_sub(wb, c0 + 64, 64), k1, sslice, op=OP.mult)
                op("vector", None, "tensor_tensor", ap_sub(wc, c0, 64), k2, sslice, op=OP.mult)
                op("vector", None, "tensor_tensor", ap_sub(wc, c0 + 64, 64), k2, cslice, op=OP.mult)
            f_n = 0
            for hh in range(4):
                c0 = hh * 128
                op("vector", None, "tensor_tensor", ap_sub(wd, c0, 64),
                   ap_sub(wb, c0, 64), ap_sub(wc, c0, 64), op=OP.subtract)
                f_n = op("vector", sV if hh == 3 else None, "tensor_tensor",
                         ap_sub(wd, c0 + 64, 64),
                         ap_sub(wb, c0 + 64, 64), ap_sub(wc, c0 + 64, 64), op=OP.add)
            wait("vector", sV, f_n)
            op("vector", None, "tensor_scalar", wb, wd, 0.0, None, op0=OP.max)
            v_n = op("vector", sV, "tensor_scalar", wc, wd, 0.0, None, op0=OP.min)
            wait("scalar", sV, v_n)
            s_n = op("scalar", sS, "activation", wc, wc, AF.Exp)
            wait("vector", sS, s_n)
            return op("vector", sV, "tensor_tensor", wa, wc, wb, op=OP.add)

        wks_spill = {0: 0, 1: 0}

        def emit_tm_pass(w_dram, bias_dram, kind, spill_dram):
            wait("sync", sT, sT.n)
            wait("sync", sV, sV.n)
            wait("sync", sG, sG.n)
            bb_done = emit_bias_bcast(bias_dram)
            last_mm = {}
            cons_lm = {}
            for oh in range(4):
                if oh > 0:
                    wait("sync", sT, last_mm[(oh - 1, NT - 1)])
                ld = 0
                for k in range(KC):
                    ld = dma("sync", sL, wslot(k),
                             dram_ap(w_dram, k * 128 * D + oh * 512, [[D, 128], [1, 512]]))
                wait("tensor", sL, ld)
                wait("tensor", sV, ln1_done)
                for t in range(NT):
                    p = pb[t % 2]
                    use_bank(t % 2)
                    for k in range(KC):
                        last_mm[(oh, t)] = op(
                            "tensor", sT if k == KC - 1 else None, "matmul",
                            p[0:128, 0:512], a1(k * 1024 + t * 128, 128), wslot(k),
                            start=(k == 0), stop=(k == KC - 1))
                    tp = t % 2
                    wa, wb, wc, wd = (wks(tp, 0), wks(tp, 1), wks(tp, 2), wks(tp, 3))
                    wait("vector", sT, last_mm[(oh, t)])
                    wait("vector", sL, bb_done)
                    wait("vector", sG, wks_spill[tp])
                    op("vector", None, "tensor_tensor", wa, p[0:128, 0:512],
                       a3(oh * 512, 512), op=OP.add)
                    if kind == "v":
                        fin = op("vector", sV, "nop")
                    else:
                        fin = emit_rope_elu_tm(wa, wb, wc, wd, t)
                    rd_bank(t % 2, sV, fin)
                    cons_lm[(oh, t)] = fin
                    if kind in ("k", "v"):
                        wait("gpsimd", sV, fin)
                        wks_spill[tp] = dma(
                            "gpsimd", sG,
                            dram_ap(spill_dram, t * 128 * D + oh * 512, [[D, 128], [1, 512]]),
                            wa)
                    else:
                        for hh in range(4):
                            bank = 4 + hh % 2
                            wait("tensor", sV, fin)
                            use_bank(bank)
                            t_n = op("tensor", sT, "transpose", pb[bank][0:128, 0:128],
                                     ap_sub(wa, hh * 128, 128), idn[:])
                            wait("vector", sT, t_n)
                            v_n = op("vector", sV, "tensor_copy",
                                     ap_sub(wb, hh * 128, 128), pb[bank][0:128, 0:128])
                            rd_bank(bank, sV, v_n)
                        cons_lm[(oh, t)] = sV.n
                        rd_bank(t % 2, sV, cons_lm[(oh, t)])
                        wait("gpsimd", sV, cons_lm[(oh, t)])
                        wks_spill[tp] = dma(
                            "gpsimd", sG,
                            dram_ap(spill_dram,
                                    (oh * 4) * NT * 128 * 128 + t * 128 * 128,
                                    [[128, 128], [NT * 128 * 128, 4], [1, 128]]),
                            bass.AP(wb.tensor, wb.offset, [list(wb.ap[0]), [128, 4], [1, 128]]))
            return sG.n

        emit_tm_pass(wk, biases["k"], "k", k_sp)
        emit_tm_pass(wv, biases["v"], "v", v_sp)
        kv_spill_done = sG.n

        if stage < 5:
            wait("gpsimd", sG, kv_spill_done)
            dma("gpsimd", sG, dram_ap(dbg["k"], 0, [[1, NT * 128 * D]]),
                dram_ap(k_sp, 0, [[1, NT * 128 * D]]))
            dma("gpsimd", sG, dram_ap(dbg["v"], 0, [[1, NT * 128 * D]]),
                dram_ap(v_sp, 0, [[1, NT * 128 * D]]))
        if stage == 1:
            dma("gpsimd", sG, dram_ap(dbg["tabs"], 0, [[1024, 128], [1, 512]]), stm[:])
            dma("gpsimd", sG, dram_ap(dbg["tabs"], 512, [[1024, 128], [1, 512]]), ctm[:])

        def kr(tp, off=0, n=2048, p=128, pbase=0):
            return bass.AP(A2[:].tensor, 8192 + tp * 2048 + off + pbase * A2W,
                           [[A2W, p], [1, n]])

        def vr(tp, off=0, n=2064, p=128, pbase=0):
            return bass.AP(A2[:].tensor, 12288 + tp * 2064 + off + pbase * A2W,
                           [[A2W, p], [1, n]])

        wait("vector", sG, kv_spill_done)
        for tp in range(2):
            for h in range(H):
                op("vector", None, "memset", vr(tp, h * 129 + 128, 1), 1.0)
        op("vector", None, "memset", S_all[:], 0.0)
        vring_ones = op("vector", sV, "nop")

        p3_mm = {}
        p3_add = {}
        for tau in range(NT):
            tp = tau % 2
            if tau >= 2:
                wait("sync", sT, p3_mm[(tau - 2, H - 1)])
            wait("sync", sV, vring_ones)
            dma("sync", sL, kr(tp), dram_ap(k_sp, tau * 128 * D, [[D, 128], [1, D]]))
            ld = dma("sync", sL,
                     bass.AP(A2[:].tensor, 12288 + tp * 2064, [[A2W, 128], [129, 16], [1, 128]]),
                     dram_ap(v_sp, tau * 128 * D, [[D, 128], [128, 16], [1, 128]]))
            wait("tensor", sL, ld)
            for h in range(H):
                bi = 4 + (tau * H + h) % 2
                use_bank(bi)
                p3_mm[(tau, h)] = op("tensor", sT, "matmul", pb[bi][0:128, 0:129],
                                     kr(tp, h * 128, 128), vr(tp, h * 129, 129),
                                     start=True, stop=True)
                wait("vector", sT, p3_mm[(tau, h)])
                p3_add[tau * H + h] = op(
                    "vector", sV, "tensor_tensor", S_all[:, h * 129:(h + 1) * 129],
                    S_all[:, h * 129:(h + 1) * 129], pb[bi][0:128, 0:129], op=OP.add)
                rd_bank(bi, sV, p3_add[tau * H + h])
        p3_done = sV.n

        wait("gpsimd", sV, p3_done)
        dma("gpsimd", sG, dram_ap(cc_in.ap(), 0, [[129, 128], [129 * 128, 16], [1, 129]]),
            bass.AP(S_all[:].tensor, 0, [[H * 129, 128], [129, 16], [1, 129]]))
        wait("gpsimd", sG)

        def f_cc(e):
            e.collective_compute(
                "AllGather", OP.bypass,
                replica_groups=[list(range(NCORES))],
                ins=[cc_in.ap().opt()],
                outs=[cc_out.ap().opt()],
            ).then_inc(sem_h[id(sC)], 1)
        emit("gpsimd", f_cc)
        sC.n += 1

        if stage in (2, 3, 4):
            dma("gpsimd", sG, dram_ap(dbg["S"], 0, [[129, 128], [129 * 128, 16], [1, 129]]),
                bass.AP(S_all[:].tensor, 0, [[H * 129, 128], [129, 16], [1, 129]]))

        def emit_fm_pass(w_dram, bname, spill_dram, func):
            bc = {"q": 0, "k": 16, "v": 32, "g": 48, "o": 64, "f": 80}[bname]
            mm_lm = {}
            sp_lm = {}
            act_lm = {}
            wait("sync", sT, sT.n)
            wait("sync", sV, sV.n)
            for oc in range(H):
                sl = oc % 2
                if oc >= 2:
                    wait("sync", sT, mm_lm[(oc - 2, 1)])
                ld = 0
                for k in range(KC):
                    ld = dma("sync", sL, a3(sl * 2048 + k * 128, 128),
                             dram_ap(w_dram, k * 128 * D + oc * 128, [[D, 128], [1, 128]]))
                wait("tensor", sL, ld)
                wait("tensor", sV, ln1_done)
                for th in range(2):
                    p = pb[2 + th]
                    use_bank(2 + th)
                    for k in range(KC):
                        mm_lm[(oc, th)] = op(
                            "tensor", sT if k == KC - 1 else None, "matmul",
                            p[0:128, 0:512], a3(sl * 2048 + k * 128, 128),
                            a1(k * 1024 + th * 512, 512),
                            start=(k == 0), stop=(k == KC - 1))
                    wait("scalar", sT, mm_lm[(oc, th)])
                    if oc >= 1:
                        wait("scalar", sG, sp_lm.get((oc - 1, th), 0))
                    act_lm[(oc, th)] = op("scalar", sS, "activation", a2(th * 512, 512),
                                          p[0:128, 0:512], func,
                                          bias=bcol[:, bc + oc:bc + oc + 1])
                    rd_bank(2 + th, sS, act_lm[(oc, th)])
                    wait("gpsimd", sS, act_lm[(oc, th)])
                    sp_lm[(oc, th)] = dma(
                        "gpsimd", sG,
                        dram_ap(spill_dram, oc * NT * 128 * 128 + th * 4 * 128 * 128,
                                [[128, 128], [128 * 128, 4], [1, 128]]),
                        bass.AP(A2[:].tensor, th * 512, [[A2W, 128], [128, 4], [1, 128]]))

        emit_fm_pass(wg, "g", g_sp, AF.Sigmoid)
        emit_tm_pass(wq, biases["q"], "q", q_sp)
        qg_spill_done = sG.n
        q_tensor_done = sT.n

        if stage < 5:
            wait("gpsimd", sG, qg_spill_done)
            dma("gpsimd", sG, dram_ap(dbg["q"], 0, [[1, H * NT * 128 * 128]]),
                dram_ap(q_sp, 0, [[1, H * NT * 128 * 128]]))
            dma("gpsimd", sG, dram_ap(dbg["g"], 0, [[1, H * NT * 128 * 128]]),
                dram_ap(g_sp, 0, [[1, H * NT * 128 * 128]]))
        if stage == 1:
            wait("gpsimd", sC, 1)
            wait("gpsimd", sG)
            finish()
            return nc

        pre_seed_v = sV.n
        all_spills = sG.n
        op("vector", None, "memset", S_sb[:], 0.0)
        wait("gpsimd", sC, 1)
        wait("gpsimd", sT, q_tensor_done)

        def stslot(cp):
            return bass.AP(A1[:].tensor, (cp % 2) * 2064,
                           [[A1W, 128], [129, 16], [1, 129]])

        def gsrc(cp):
            return dram_ap(cc_out.ap(), cp * D * 129,
                           [[129, 128], [129 * 128, 16], [1, 129]])

        pm_ld = dma("gpsimd", sG, pmc8[:],
                    bass.AP(pmask_d.tensor, 0, [[0, 128], [1, NCORES]]))
        g_lms = [0] * NCORES
        for cp in (0, 1):
            g_lms[cp] = dma("gpsimd", sG, stslot(cp), gsrc(cp))
        wait("vector", sG, pm_ld)
        cons_lm = 0
        for cp in range(NCORES):
            wait("vector", sG, g_lms[cp])
            cons_lm = op("vector", sV, "scalar_tensor_tensor", S_sb[:],
                         stslot(cp), pmc8[:, cp:cp + 1], S_sb[:],
                         op0=OP.mult, op1=OP.add)
            if cp + 2 < NCORES:
                wait("gpsimd", sV, cons_lm)
                g_lms[cp + 2] = dma("gpsimd", sG, stslot(cp + 2), gsrc(cp + 2))
        seed_done = cons_lm

        if stage == 2:
            wait("gpsimd", sV, seed_done)
            dma("gpsimd", sG, dram_ap(dbg["Ssb"], 0, [[129, 128], [129 * 128, 16], [1, 129]]),
                bass.AP(S_sb[:].tensor, 0, [[H * 129, 128], [129, 16], [1, 129]]))
        if stage == 2:
            wait("gpsimd", sG)
            finish()
            return nc

        def fwh(h):
            return dram_ap(fw, (h // 2) * 128 * D + (h % 2) * 1024, [[D, 128], [1, 1024]])

        def a4(off, n):
            return bass.AP(A4[:].tensor, off, [[2048, 128], [1, n]])

        def sall4(off, n=1024):
            return bass.AP(S_all[:].tensor, off, [[H * 129, 128], [1, n]])

        sc_ld = {}
        sc_lm = {}
        t_ld = {}
        t_g = {}
        th_dv = [0]
        pa_v, pb_v = pad_scratch["vector"]
        pa_s, pb_s = pad_scratch["scalar"]
        if stage >= 5:
            op("vector", None, "memset", accc[:], 0.0)
            for h in (0, 1):
                sc_ld[h] = dma("gpsimd", sG, a4((h % 2) * 1024, 1024), fwh(h))

        def ffn_scale_half(h):
            wait("vector", sG, sc_ld[h])
            op("vector", None, "tensor_reduce", rcol[:], a4((h % 2) * 1024, 1024),
               axis=AX.X, op=OP.add, apply_absolute_value=True)
            pad("vector")
            pad("vector")
            pad("vector")
            op("vector", None, "tensor_copy", rcol2[:], rcol[:])
            pad("vector")
            pad("vector")
            lm = op("vector", sV, "tensor_tensor", accc[:], accc[:], rcol2[:], op=OP.add)
            sc_lm[h] = lm
            if h + 2 < 32:
                wait("gpsimd", sV, lm)
                sc_ld[h + 2] = dma("gpsimd", sG, a4((h % 2) * 1024, 1024), fwh(h + 2))

        def ffn_threshold():
            pad("vector")
            pad("vector")
            acc_done = op("vector", sV, "tensor_copy", pb_v, pa_v)
            wait("sync", sV, acc_done)
            l1 = dma("sync", sL, dram_ap(bc_d, 5120, [[1, 1], [1, 128]]), accc[:])
            wait("sync", sL, l1)
            l2 = dma("sync", sL, trow[:], dram_ap(bc_d, 5120, [[1, 1], [1, 128]]))
            wait("vector", sL, l2)
            op("vector", None, "tensor_reduce", tot[:], trow[:], axis=AX.X, op=OP.add)
            pad("vector")
            pad("vector")
            tot_n = op("vector", sV, "tensor_copy", pb_v, pa_v)
            wait("scalar", sV, tot_n)
            op("scalar", None, "activation", t11a[:], tot[:], AF.Identity,
               bias=c_e11[:], scale=1.0 / (1 << 23))
            op("scalar", None, "activation", t11s[:], tot[:], AF.Identity,
               scale=1.0 / (1 << 22))
            op("scalar", None, "activation", t11n[:], t11a[:], AF.Identity, scale=-1.0)
            pad("scalar")
            pad("scalar")
            s_n = op("scalar", sS, "activation", pb_s, pa_s, AF.Identity)
            th = 0
            for src, dst, boff in ((t11a, thc, 5504), (t11n, nthc, 5760),
                                   (t11s, scc, 5888)):
                th = emit_row_bcast(src[:], dst[:], 1, boff, sS, s_n)
            return th

        def ffn_tern_half(h):
            wait("vector", sG, t_ld[h])
            wait("vector", sL, th_dv[0])
            if h >= 1:
                wait("vector", sG, t_g[h - 1])
            op("vector", None, "tensor_scalar", sall4(0),
               a4((h % 2) * 1024, 1024), thc[:], None, op0=OP.is_gt)
            op("vector", None, "tensor_scalar", sall4(1024),
               a4((h % 2) * 1024, 1024), nthc[:], None, op0=OP.is_lt)
            pad("vector")
            op("vector", None, "tensor_tensor", sall4(0), sall4(0),
               sall4(1024), op=OP.subtract)
            pad("vector")
            pad("vector")
            lm = op("vector", sV, "tensor_copy", pb_v, pa_v)
            wait("gpsimd", sV, lm)
            t_g[h] = dma("gpsimd", sG,
                         dram_ap(wt_sp, (h // 2) * 128 * D + (h % 2) * 1024,
                                 [[D, 128], [1, 1024]]),
                         sall4(0))
            if h + 2 < 32:
                t_ld[h + 2] = dma("gpsimd", sG, a4((h % 2) * 1024, 1024), fwh(h + 2))

        def ffn_prep(b):
            if stage < 5:
                return
            if b < 8:
                for j in range(4):
                    ffn_scale_half(4 * b + j)
                return
            if b == 8:
                th_dv[0] = ffn_threshold()
                wait("gpsimd", sV, sc_lm[31])
                for h in (0, 1):
                    t_ld[h] = dma("gpsimd", sG, a4((h % 2) * 1024, 1024), fwh(h))
            for j in range(4):
                ffn_tern_half(4 * (b - 8) + j)

        def qr(tp, off=0, n=128, p=128, pbase=0):
            return bass.AP(A2[:].tensor, tp * 2048 + off + pbase * A2W,
                           [[A2W, p], [1, n]])

        def gr(tp, off=0, n=64):
            return bass.AP(A2[:].tensor, 4096 + tp * 2048 + off, [[A2W, 128], [1, n]])

        gate_lm = {}
        add_lm = {}
        att_lm = {}
        scan_ld = [0] * NT
        blk_gate = pre_seed_v
        blk_tr = 0
        for tau in range(NT):
            tp = tau % 2
            if tau >= 2:
                wait("sync", sV, gate_lm[(tau - 2, 1)])
            else:
                wait("sync", sG, all_spills)
                wait("sync", sT, q_tensor_done)
            dma("sync", sL, bass.AP(A2[:].tensor, tp * 2048, [[A2W, 128], [128, 16], [1, 128]]),
                dram_ap(q_sp, tau * 128 * 128, [[128, 128], [NT * 128 * 128, 16], [1, 128]]))
            dma("sync", sL, bass.AP(A2[:].tensor, 4096 + tp * 2048,
                                    [[A2W, 128], [128, 16], [1, 128]]),
                dram_ap(g_sp, tau * 128 * 128, [[128, 128], [NT * 128 * 128, 16], [1, 128]]))
            dma("sync", sL, kr(tp), dram_ap(k_sp, tau * 128 * D, [[D, 128], [1, D]]))
            scan_ld[tau] = dma(
                "sync", sL,
                bass.AP(A2[:].tensor, 12288 + tp * 2064, [[A2W, 128], [129, 16], [1, 128]]),
                dram_ap(v_sp, tau * 128 * D, [[D, 128], [128, 16], [1, 128]]))
            wait("tensor", sL, scan_ld[tau])
            for ic in range(2):
                add_half = blk_gate
                for half in range(2):
                    wait("tensor", sV, add_half)
                    ds_last = 0
                    for b in range(8):
                        h = half * 8 + b
                        ds_last = op("tensor", sT if b == 7 else None, "matmul",
                                     pb[b][0:128, 0:129],
                                     kr(tp, h * 128, 128, p=64, pbase=ic * 64),
                                     vr(tp, h * 129, 129, p=64, pbase=ic * 64),
                                     start=True, stop=True)
                    wait("vector", sT, ds_last)
                    for b in range(8):
                        h = half * 8 + b
                        add_half = op("vector", sV if b == 7 else None, "tensor_tensor",
                                      S_sb[:, h * 129:(h + 1) * 129],
                                      S_sb[:, h * 129:(h + 1) * 129],
                                      pb[b][0:128, 0:129], op=OP.add)
                pad("vector")
                pad("vector")
                pa, pb2 = pad_scratch["vector"]
                add_last = op("vector", sV, "tensor_copy", pb2, pa)
                ev_last = add_last
                for half in range(2):
                    wait("tensor", sV, ev_last)
                    nm_last = 0
                    for b in range(8):
                        h = half * 8 + b
                        nm_last = op("tensor", sT if b == 7 else None, "matmul",
                                     pb[b][0:64, 0:129],
                                     qr(tp, h * 128 + ic * 64, 64),
                                     S_sb[:, h * 129:(h + 1) * 129],
                                     start=True, stop=True)
                    wait("vector", sT, nm_last)
                    for b in range(8):
                        h = half * 8 + b
                        ev_last = op("vector", sV if b == 7 else None, "tensor_copy",
                                     bass.AP(A3[:].tensor, h * 129, [[A3W, 64], [1, 129]]),
                                     pb[b][0:64, 0:129])
                pad("vector")
                op("vector", None, "tensor_scalar", dnall[:],
                   bass.AP(A3[:].tensor, 128, [[A3W, 64], [129, 16]]),
                   1e-6, None, op0=OP.add)
                pad("vector")
                op("vector", None, "reciprocal", rcall[:], dnall[:])
                pad("vector")
                att_last = 0
                for h in range(H):
                    att_last = op("vector", sV if h == H - 1 else None, "tensor_tensor",
                                  bass.AP(A3[:].tensor, 2064 + h * 128, [[A3W, 64], [1, 128]]),
                                  bass.AP(A3[:].tensor, h * 129, [[A3W, 64], [1, 128]]),
                                  bass.AP(rcall[:].tensor, h, [[16, 64], [0, 128]]),
                                  op=OP.mult)
                att_lm[(tau, ic)] = att_last
                if stage in (3, 4) and tau == 0 and ic == 0:
                    wait("vector", sG, s_cp)
                    op("vector", None, "tensor_copy", dbg_h0_sb[:], S_sb[0:128, 0:256])
                    op("vector", None, "tensor_copy",
                       bass.AP(dbg_num_sb[:].tensor, 0, [[516, 64], [1, 129]]),
                       bass.AP(A3[:].tensor, 0, [[A3W, 64], [1, 129]]))
                    op("vector", None, "tensor_copy",
                       bass.AP(dbg_num_sb[:].tensor, 129, [[516, 64], [1, 16]]),
                       dnall[:])
                    op("vector", None, "tensor_copy",
                       bass.AP(dbg_num_sb[:].tensor, 145, [[516, 64], [1, 16]]),
                       rcall[:])
                    op("vector", None, "tensor_copy",
                       bass.AP(dbg_num_sb[:].tensor, 161, [[516, 64], [1, 128]]),
                       bass.AP(A3[:].tensor, 2064, [[A3W, 64], [1, 128]]))
                g_last = 0
                for half in range(2):
                    wait("tensor", sV, att_last if half == 0 else g_last)
                    tr_last = 0
                    for b in range(8):
                        h = half * 8 + b
                        tr_last = op("tensor", sT if b == 7 else None, "transpose",
                                     pb[b][0:128, 0:64],
                                     bass.AP(A3[:].tensor, 2064 + h * 128,
                                             [[A3W, 64], [1, 128]]),
                                     idn[0:64, 0:64])
                    wait("vector", sT, tr_last)
                    for b in range(8):
                        h = half * 8 + b
                        g_last = op("vector", sV if b == 7 else None, "tensor_tensor",
                                    a1(h * 1024 + tau * 128 + ic * 64, 64),
                                    pb[b][0:128, 0:64],
                                    gr(tp, h * 128 + ic * 64, 64),
                                    op=OP.mult)
                blk_gate = g_last
                gate_lm[(tau, ic)] = g_last
                ffn_prep(tau * 2 + ic)
        scan_done = sV.n

        if stage in (3, 4):
            wait("gpsimd", sV, scan_done)
            dma("gpsimd", sG, dram_ap(dbg["num"], 0, [[129 * 4, 64], [1, 129 * 4]]),
                bass.AP(dbg_num_sb[:].tensor, 0, [[129 * 4, 64], [1, 129 * 4]]))
            dma("gpsimd", sG, dram_ap(dbg["h0"], 0, [[256, 128], [1, 256]]),
                dbg_h0_sb[:])
            dma("gpsimd", sG, dram_ap(dbg["Ssb"], 0, [[129, 128], [129 * 128, 16], [1, 129]]),
                bass.AP(S_sb[:].tensor, 0, [[H * 129, 128], [129, 16], [1, 129]]))
            for k in range(KC):
                dma("gpsimd", sG, dram_ap(dbg["attn"], k * 128 * T, [[T, 128], [1, T]]),
                    a1(k * 1024, 1024))
        if stage == 3:
            wait("gpsimd", sG)
            finish()
            return nc

        wo_mm = {}
        wo_fin = {}
        wait("sync", sV, scan_done)
        for oc in range(H):
            sl = oc % 2
            if oc >= 2:
                wait("sync", sT, wo_mm[(oc - 2, 1)])
            if oc >= 1:
                wait("sync", sV, wo_fin.get((oc - 1) * 2 + 1, 0))
            for k in range(KC):
                dma("sync", sL, a3(sl * 2048 + k * 128, 128),
                    dram_ap(wo, k * 128 * D + oc * 128, [[D, 128], [1, 128]]))
            for th in range(2):
                dma("sync", sL, a3(4096 + th * 512, 512),
                    dram_ap(x_t, oc * 128 * T + th * 512, [[T, 128], [1, 512]]))
            ld = sL.n
            wait("tensor", sL, ld)
            wait("tensor", sV, scan_done)
            for th in range(2):
                bi = (oc * 2 + th) % 4
                use_bank(bi)
                for k in range(KC):
                    wo_mm[(oc, th)] = op(
                        "tensor", sT if k == KC - 1 else None, "matmul",
                        pb[bi][0:128, 0:512], a3(sl * 2048 + k * 128, 128),
                        a1(k * 1024 + th * 512, 512),
                        start=(k == 0), stop=(k == KC - 1))
                wait("scalar", sT, wo_mm[(oc, th)])
                if oc >= 1:
                    wait("scalar", sV, wo_fin.get((oc - 1) * 2 + th, 0))
                s_n = op("scalar", sS, "activation", a3(5120 + th * 512, 512),
                         pb[bi][0:128, 0:512], AF.Identity, bias=bcol[:, 64 + oc:64 + oc + 1])
                rd_bank(bi, sS, s_n)
                wait("vector", sS, s_n)
                wait("vector", sL, ld)
                wo_fin[oc * 2 + th] = op(
                    "vector", sV, "tensor_tensor", a2(oc * 1024 + th * 512, 512),
                    a3(5120 + th * 512, 512), a3(4096 + th * 512, 512), op=OP.add)
        x2_done = sV.n

        if stage == 4:
            wait("gpsimd", sV, x2_done)
            for k in range(KC):
                dma("gpsimd", sG, dram_ap(dbg["x2"], k * 128 * T, [[T, 128], [1, T]]),
                    a2(k * 1024, 1024))
            wait("gpsimd", sG)
            finish()
            return nc

        wait("tensor", sV, x2_done)
        wait("scalar", sV, x2_done)
        ln2_done = emit_ln(
            lambda k: a2(k * 1024, 1024), lambda k: a1(k * 1024, 1024),
            None,
            [a1(0, 1024), a1(1024, 1024)],
            lambda off, n=512: a3row(2048 + off, n),
            [a3(0, 1024), a3(1024, 1024)],
            False, 3072,
        )

        slot_mm = {0: 0, 1: 0}
        ff_fin = {}
        st_lm = {0: 0, 1: 0}
        wait("sync", sG, t_g[31])
        wait("sync", sV, ln2_done)
        wait("tensor", sV, ln2_done)
        for pi in range(4):
            th, og = pi // 2, pi % 2
            for k in range(KC):
                slw = k % 2
                wait("sync", sT, slot_mm[slw])
                ld = dma("sync", sL, a3(slw * 1024, 1024),
                         dram_ap(wt_sp, k * 128 * D + og * 1024, [[D, 128], [1, 1024]]))
                wait("tensor", sL, ld)
                for oc in range(8):
                    if k == 0:
                        use_bank(oc)
                    slot_mm[slw] = op(
                        "tensor", sT if oc == 7 else None, "matmul",
                        pb[oc][0:128, 0:512], a3(slw * 1024 + oc * 128, 128),
                        a1(k * 1024 + th * 512, 512),
                        start=(k == 0), stop=(k == KC - 1))
            for oc in range(8):
                o_g = og * 8 + oc
                ft = bass.AP(ftmp[:].tensor, (oc % 2) * 512, [[1024, 128], [1, 512]])
                wait("scalar", sT, slot_mm[(KC - 1) % 2])
                wait("scalar", sL, th_dv[0])
                if pi > 0 or oc >= 2:
                    wait("scalar", sG, st_lm[oc % 2])
                s_n = op("scalar", sS, "activation", ft, pb[oc][0:128, 0:512],
                         AF.Identity, bias=bcol[:, 80 + o_g:80 + o_g + 1],
                         scale=scc[:])
                rd_bank(oc, sS, s_n)
                wait("vector", sS, s_n)
                op("vector", None, "tensor_tensor", ft, ft,
                   a2(o_g * 1024 + th * 512, 512), op=OP.add)
                pad("vector")
                pad("vector")
                ff_fin[(pi, oc)] = op("vector", sV, "tensor_copy", pb_v, pa_v)
                wait("gpsimd", sV, ff_fin[(pi, oc)])
                st_lm[oc % 2] = dma("gpsimd", sG,
                                    dram_ap(y_t, o_g * 128 * T + th * 512, [[T, 128], [1, 512]]),
                                    ft)

        wait("gpsimd", sG)
        finish()
    return nc


_ROPE_PERM = None


def _rope_perm():
    global _ROPE_PERM
    if _ROPE_PERM is None:
        p = []
        for h in range(H):
            base = h * DK
            p.extend(base + np.arange(0, DK, 2))
            p.extend(base + np.arange(1, DK, 2))
        _ROPE_PERM = np.asarray(p)
    return _ROPE_PERM


def _host_prep(inputs):
    perm = _rope_perm()
    f32 = np.float32
    x = np.asarray(inputs["L0_attn_x"], f32)
    wqT = np.ascontiguousarray(np.asarray(inputs["L0_attn_Wq"], f32).T[:, perm])
    wkT = np.ascontiguousarray(np.asarray(inputs["L0_attn_Wk"], f32).T[:, perm])
    wvT = np.ascontiguousarray(np.asarray(inputs["L0_attn_Wv"], f32).T)
    wgT = np.ascontiguousarray(np.asarray(inputs["L0_attn_Wg"], f32).T)
    woT = np.ascontiguousarray(np.asarray(inputs["L0_attn_Wo"], f32).T)
    fwT = np.ascontiguousarray(np.asarray(inputs["L0_ffn_weight"], f32).T)
    bq = np.ascontiguousarray(np.asarray(inputs["L0_attn_bq"], f32)[perm])
    bk = np.ascontiguousarray(np.asarray(inputs["L0_attn_bk"], f32)[perm])
    ident = np.eye(128, dtype=f32)
    xT = np.ascontiguousarray(x[0].T)
    in_maps = []
    for c in range(NCORES):
        in_maps.append(dict(
            x_t=np.ascontiguousarray(xT[:, c * T:(c + 1) * T]),
            wq=wqT, wk=wkT, wv=wvT, wg=wgT, wo=woT, fw=fwT,
            bq=bq, bk=bk,
            bv=np.asarray(inputs["L0_attn_bv"], f32),
            bg=np.asarray(inputs["L0_attn_bg"], f32),
            bo=np.asarray(inputs["L0_attn_bo"], f32),
            bf=np.asarray(inputs["L0_ffn_bias"], f32),
            pos=np.arange(c * T, (c + 1) * T, dtype=f32).reshape(1, T),
            jrow=np.arange(64, dtype=f32).reshape(1, 64),
            ident=ident,
            pmask=(np.arange(NCORES) < c).astype(f32).reshape(1, NCORES),
        ))
    return in_maps


def kernel(**inputs):
    in_maps = _host_prep(inputs)
    nc = _build(stage=5)
    res = run_bass_kernel_spmd(nc, in_maps, core_ids=list(range(NCORES)))
    out = np.empty((1, NCORES * T, D), np.float32)
    for c in range(NCORES):
        out[0, c * T:(c + 1) * T, :] = res.results[c]["y_t"].T
    return out
